# version 04
# baseline (speedup 1.0000x reference)
import contextlib
import numpy as np
import ml_dtypes
import concourse.bass as bass
import concourse.mybir as mybir
from concourse.bass_utils import run_bass_kernel_spmd

F32 = mybir.dt.float32
BF16 = mybir.dt.bfloat16
I32 = mybir.dt.int32
AF = mybir.ActivationFunctionType
ALU = mybir.AluOpType
AX = mybir.AxisListType
BF = ml_dtypes.bfloat16

B, S, D, T = 4, 2048, 4096, 8192
NCORE, OWNP, OWNT = 8, 256, 1024
HQ, HKV, DH, LAT, KVD = 16, 4, 64, 1024, 256
IFF, NI, NF = 11008, 86, 32
EPS = 1e-5
NT, TW = 16, 512

_DBG = []
_PHASES = ("A", "B", "C", "D1", "E", "D2")



def _dei64():
    p = np.zeros(64, np.int64)
    p[:32] = np.arange(32) * 2
    p[32:] = np.arange(32) * 2 + 1
    return p


def _qperm():
    d = _dei64()
    return np.concatenate([h * 64 + d for h in range(HQ)])


def _kperm():
    d = _dei64()
    return np.concatenate([g * 64 + d for g in range(HKV)])


def _dense_conv_T(w, groups, perm):
    O, Ig, K = w.shape
    out = np.zeros((K, O, O), np.float32)
    for tap in range(K):
        for c in range(O):
            g = c // (O // groups)
            out[tap, g * Ig:(g + 1) * Ig, c] = w[c, :, tap]
    return np.ascontiguousarray(out[:, perm][:, :, perm])


def host_prep(inp):
    x = np.asarray(inp["L0_attn_a_x"], np.float32)
    qp, kp = _qperm(), _kperm()

    x_t = np.ascontiguousarray(x.transpose(2, 0, 1).reshape(D, T))

    w_qkv = np.asarray(inp["L0_attn_a_w_qkv"], np.float32)
    wa_qT = np.ascontiguousarray(w_qkv.T[:, qp])
    kv_cols = np.concatenate([1024 + kp, np.arange(1280, 1536)])
    wa_kvT = np.ascontiguousarray(w_qkv.T[:, kv_cols])
    wa_oT = np.ascontiguousarray(np.asarray(inp["L0_attn_a_w_o"], np.float32).T)

    qc1dT = _dense_conv_T(np.asarray(inp["L0_attn_a_qc1_w"], np.float32), HQ, qp)
    qc2dT = _dense_conv_T(np.asarray(inp["L0_attn_a_qc2_w"], np.float32), 1, qp)
    kc1dT = _dense_conv_T(np.asarray(inp["L0_attn_a_kc1_w"], np.float32), HKV, kp)
    kc2dT = _dense_conv_T(np.asarray(inp["L0_attn_a_kc2_w"], np.float32), 1, kp)
    qc1b = np.asarray(inp["L0_attn_a_qc1_b"], np.float32)[qp]
    qc2b = np.asarray(inp["L0_attn_a_qc2_b"], np.float32)[qp]
    kc1b = np.asarray(inp["L0_attn_a_kc1_b"], np.float32)[kp]
    kc2b = np.asarray(inp["L0_attn_a_kc2_b"], np.float32)[kp]

    inv = 1.0 / 10000.0 ** (np.arange(0, DH, 2, dtype=np.float32) / DH)
    ang = np.arange(S, dtype=np.float32)[:, None] * inv[None, :]
    scos, ssin = np.cos(ang).T.astype(np.float32), np.sin(ang).T.astype(np.float32)
    r32 = np.arange(128) % 32
    sgn = np.where(((np.arange(128) // 32) % 2) == 0, -1.0, 1.0).astype(np.float32)
    cos2 = np.ascontiguousarray(np.tile(scos[r32], (1, B)))
    sin2 = np.ascontiguousarray(np.tile(ssin[r32], (1, B)) * sgn[:, None])

    wb_qkv = np.asarray(inp["L0_attn_b_weight_qkv"], np.float32)
    cols = np.zeros(6144, np.int64)
    for m in range(4096):
        h, dd = m // 128, m % 128
        cols[m] = (h // 4) * 768 + (h % 4) * 128 + dd
    for m in range(1024):
        g, dd = m // 128, m % 128
        cols[4096 + m] = g * 768 + 512 + dd
        cols[5120 + m] = g * 768 + 640 + dd
    wb_qkvT = np.ascontiguousarray(wb_qkv.T[:, cols])
    wb_projT = np.ascontiguousarray(np.asarray(inp["L0_attn_b_weight_proj"], np.float32).T)

    wgT = np.ascontiguousarray(np.asarray(inp["L0_ffn_w_gate"], np.int32).T)
    wuT = np.ascontiguousarray(np.asarray(inp["L0_ffn_w_up"], np.int32).T)
    wdT = np.ascontiguousarray(np.asarray(inp["L0_ffn_w_down"], np.int32).T)
    sgTc = np.ascontiguousarray(np.asarray(inp["L0_ffn_s_gate"], np.float32).reshape(NI, 128).T)
    suTc = np.ascontiguousarray(np.asarray(inp["L0_ffn_s_up"], np.float32).reshape(NI, 128).T)
    sdTc = np.ascontiguousarray(np.asarray(inp["L0_ffn_s_down"], np.float32).reshape(NF, 128).T)

    eye = np.eye(128, dtype=BF)
    sel64 = np.zeros((64, 128), dtype=BF)
    sel64[np.arange(128) % 64, np.arange(128)] = 1
    selkf = np.zeros((2, 128), dtype=np.float32)
    selkf[np.arange(128) // 64, np.arange(128)] = 1
    bdo = np.zeros((128, 2), dtype=BF)
    bdo[np.arange(128), np.arange(128) // 64] = 1
    hsel = np.zeros((128, 32 * 32), dtype=BF)
    for h in range(32):
        hsel[:, h * 32 + h] = 1
    seh = np.zeros((32, 32 * 128), dtype=BF)
    for h in range(32):
        seh[h, h * 128:(h + 1) * 128] = 1

    shared = dict(
        x_t=x_t, wa_qT=wa_qT, wa_kvT=wa_kvT, wa_oT=wa_oT,
        qc1dT=qc1dT, qc2dT=qc2dT, kc1dT=kc1dT, kc2dT=kc2dT,
        qc1b=np.ascontiguousarray(qc1b.reshape(8, 128).T),
        qc2b=np.ascontiguousarray(qc2b.reshape(8, 128).T),
        kc1b=np.ascontiguousarray(kc1b.reshape(2, 128).T),
        kc2b=np.ascontiguousarray(kc2b.reshape(2, 128).T),
        key_temp=np.asarray(inp["L0_attn_a_key_temp"], np.float32).reshape(1, 1),
        cos2=cos2, sin2=sin2,
        wb_qkvT=wb_qkvT, wb_projT=wb_projT,
        wgT=wgT, wuT=wuT, wdT=wdT, sgT=sgTc, suT=suTc, sdT=sdTc,
        eye=eye, sel64=sel64, selkf=selkf, bdo=bdo, hsel=hsel, seh=seh,
    )

    percore = []
    for j in range(NCORE):
        own_pos = np.arange(OWNP * j, OWNP * (j + 1))
        own_cols = (np.arange(B)[:, None] * S + own_pos[None, :]).reshape(-1)
        x_own = np.ascontiguousarray(x_t[:, own_cols])
        xh = np.zeros((D, B * 260), np.float32)
        for b in range(B):
            lo = OWNP * j - 4
            src0 = max(lo, 0)
            off = src0 - lo
            xh[:, b * 260 + off:(b + 1) * 260] = x_t[:, b * S + src0:b * S + OWNP * (j + 1)]
        mask = np.zeros((128, 16 * 256), dtype=BF)
        tq = OWNP * j + np.arange(256)
        for i in range(16):
            tk = 128 * i + np.arange(128)
            mask[:, i * 256:(i + 1) * 256] = (tk[:, None] <= tq[None, :]).astype(BF)
        percore.append(dict(
            x_own=x_own, x_ownh=xh, mask=np.ascontiguousarray(mask),
            cos2o=np.ascontiguousarray(cos2[:, own_cols]),
            sin2o=np.ascontiguousarray(sin2[:, own_cols]),
        ))
    return shared, percore



class Ch:
    def __init__(self, handle, step=1):
        self.h, self.n, self.step = handle, 0, step

    def inc(self, instr, k=None):
        k = self.step if k is None else k
        instr.then_inc(self.h, k)
        self.n += k
        return self.n


def build_nc(dbg=(), phases=_PHASES):
    nc = bass.Bass("TRN2", target_bir_lowering=False)
    ap = {}

    def inp(name, shape, dt):
        ap[name] = nc.dram_tensor(name, shape, dt, kind="ExternalInput").ap()

    def outp(name, shape, dt):
        ap[name] = nc.dram_tensor(name, shape, dt, kind="ExternalOutput").ap()

    def scr(name, shape, dt):
        ap[name] = nc.dram_tensor(name, shape, dt, kind="Internal").ap()

    inp("x_t", [D, T], F32)
    inp("x_own", [D, OWNT], F32)
    inp("x_ownh", [D, B * 260], F32)
    inp("wa_qT", [D, LAT], F32)
    inp("wa_kvT", [D, 512], F32)
    inp("wa_oT", [LAT, D], F32)
    inp("qc1dT", [3, LAT, LAT], F32)
    inp("qc2dT", [3, LAT, LAT], F32)
    inp("kc1dT", [3, KVD, KVD], F32)
    inp("kc2dT", [3, KVD, KVD], F32)
    inp("qc1b", [128, 8], F32)
    inp("qc2b", [128, 8], F32)
    inp("kc1b", [128, 2], F32)
    inp("kc2b", [128, 2], F32)
    inp("key_temp", [1, 1], F32)
    inp("cos2", [128, T], F32)
    inp("sin2", [128, T], F32)
    inp("cos2o", [128, OWNT], F32)
    inp("sin2o", [128, OWNT], F32)
    inp("mask", [128, 16 * 256], BF16)
    inp("wb_qkvT", [D, 6144], F32)
    inp("wb_projT", [D, D], F32)
    inp("wgT", [D, IFF], I32)
    inp("wuT", [D, IFF], I32)
    inp("wdT", [IFF, D], I32)
    inp("sgT", [128, NI], F32)
    inp("suT", [128, NI], F32)
    inp("sdT", [128, NF], F32)
    inp("eye", [128, 128], BF16)
    inp("sel64", [64, 128], BF16)
    inp("selkf", [2, 128], F32)
    inp("bdo", [128, 2], BF16)
    inp("hsel", [128, 32 * 32], BF16)
    inp("seh", [32, 32 * 128], BF16)

    outp("y", [D, OWNT], F32)

    scr("K_dram", [KVD, T], BF16)
    scr("wg_bf", [D, IFF], BF16)
    scr("wu_bf", [D, IFF], BF16)
    scr("wd_bf", [IFF, D], BF16)
    scr("V_dram", [T, KVD], BF16)
    scr("x1_dram", [D, OWNT], F32)
    scr("x2_dram", [D, OWNT], F32)
    scr("x3_dram", [D, OWNT], F32)
    dbg_shapes = {"K_dram": ([KVD, T], BF16), "V_dram": ([T, KVD], BF16),
                  "x1_dram": ([D, OWNT], F32), "x2_dram": ([D, OWNT], F32),
                  "x3_dram": ([D, OWNT], F32), "qb": ([128, 48 * 1024], BF16),
                  "AOB": ([128, NF * 1024], BF16), "erows": ([2, 1024], F32)}
    for name in dbg:
        shp, dt = dbg_shapes[name]
        outp("dbg_" + name, shp, dt)

    with contextlib.ExitStack() as st:
        E = st.enter_context

        eye_s = E(nc.sbuf_tensor("eye_s", [128, 128], BF16))
        sel64_s = E(nc.sbuf_tensor("sel64_s", [64, 128], BF16))
        selkf_s = E(nc.sbuf_tensor("selkf_s", [2, 128], F32))
        bdo_s = E(nc.sbuf_tensor("bdo_s", [128, 2], BF16))
        ones_col = E(nc.sbuf_tensor("ones_col", [128, 1], BF16))
        onesf_row = E(nc.sbuf_tensor("onesf_row", [1, 128], F32))
        ktb = E(nc.sbuf_tensor("ktb", [1, 1], F32))
        ktraw = E(nc.sbuf_tensor("ktraw", [1, 1], F32))
        negB = E(nc.sbuf_tensor("negB", [128, 1], F32))
        kt2 = E(nc.sbuf_tensor("kt2", [2, 1], F32))
        epsb = E(nc.sbuf_tensor("epsb", [1, 1], F32))
        onesf_neg = E(nc.sbuf_tensor("onesf_neg", [1, 128], F32))
        statrows = E(nc.sbuf_tensor("statrows", [1, 4 * TW], F32))
        mrow = statrows[:, 0 * TW:1 * TW]
        tsq = statrows[:, 1 * TW:2 * TW]
        varrow = tsq
        arow = statrows[:, 2 * TW:3 * TW]
        crow = statrows[:, 3 * TW:4 * TW]
        a_b = E(nc.sbuf_tensor("a_b", [128, TW], F32))
        nm_b = E(nc.sbuf_tensor("nm_b", [128, TW], F32))
        stBC = contextlib.ExitStack()
        Q = stBC.enter_context(nc.sbuf_tensor("Q", [128, 8 * 1024], BF16))
        stQM = contextlib.ExitStack()
        qmT = stQM.enter_context(nc.sbuf_tensor("qmT", [128, NF * 64], BF16))
        kmT = stQM.enter_context(nc.sbuf_tensor("kmT", [128, NF * 64], BF16))
        w1pack = stQM.enter_context(nc.sbuf_tensor("w1pack", [1, 512 + 64 + LAT + 64], F32))
        w1kv = w1pack[:, 0:512]
        w1qm = w1pack[:, 512:576]
        w1q = w1pack[:, 576:576 + LAT]
        w1km = w1pack[:, 576 + LAT:576 + LAT + 64]

        pA = E(nc.psum_tensor("pA", [128, 2048], F32))
        pB = E(nc.psum_tensor("pB", [128, 1024], F32))
        pC = E(nc.psum_tensor("pC", [128, 512], F32))
        pD = E(nc.psum_tensor("pD", [128, 1024], BF16))

        sW = Ch(E(nc.semaphore("sW")), 16)
        sG = Ch(E(nc.semaphore("sG")), 16)
        sP = Ch(E(nc.semaphore("sP")), 1)
        sV = Ch(E(nc.semaphore("sV")), 1)
        sA = Ch(E(nc.semaphore("sA")), 1)
        sO = Ch(E(nc.semaphore("sO")), 1)
        sS = Ch(E(nc.semaphore("sS")), 16)
        CHANS = [sW, sG, sP, sV, sA, sO, sS]

        bk = E(nc.Block())

        def barrier():
            snap = [(c.h, c.n) for c in CHANS]
            for eng in (bk.tensor, bk.vector, bk.scalar, bk.gpsimd, bk.sync):
                @eng
                def _(e, snap=snap):
                    for h, n in snap:
                        if n:
                            e.wait_ge(h, n)

        def stats_math(ps_S, ps_Q, w, nfeat):
            qd = sP.n

            @bk.scalar
            def _(sc, qd=qd, w=w):
                sc.wait_ge(sP.h, qd)
                sc.mul(mrow[:, 0:w], ps_S, 1.0 / nfeat)
                i = sc.activation(out=tsq[:, 0:w], in_=mrow[:, 0:w], func=AF.Square)
                sA.inc(i)
            aa = sA.n

            @bk.vector
            def _(v, aa=aa, w=w):
                v.wait_ge(sA.h, aa)
                i = v.scalar_tensor_tensor(out=varrow[:, 0:w], in0=ps_Q,
                                           scalar=1.0 / nfeat, in1=tsq[:, 0:w],
                                           op0=ALU.mult, op1=ALU.subtract)
                sV.inc(i)
            vv = sV.n

            @bk.scalar
            def _(sc, vv=vv, w=w):
                sc.wait_ge(sV.h, vv)
                i = sc.activation(out=arow[:, 0:w], in_=varrow[:, 0:w], func=AF.Sqrt,
                                  bias=epsb[:])
                sA.inc(i)
            aa2 = sA.n

            @bk.vector
            def _(v, aa2=aa2, w=w):
                v.wait_ge(sA.h, aa2)
                v.reciprocal(arow[:, 0:w], arow[:, 0:w])
                i = v.tensor_scalar(out=crow[:, 0:w], in0=mrow[:, 0:w], scalar1=-1.0,
                                    scalar2=None, op0=ALU.mult)
                sV.inc(i)
            return sV.n

        @bk.sync
        def _(sy):
            for dst, src in [(eye_s, "eye"), (sel64_s, "sel64"), (selkf_s, "selkf"),
                             (bdo_s, "bdo"), (ktb, "key_temp"), (ktraw, "key_temp")]:
                i = sy.dma_start(out=dst[:], in_=ap[src])
                sW.inc(i)
        prep_w = sW.n

        @bk.gpsimd
        def _(g):
            g.memset(ones_col[:], 1.0)
            g.memset(epsb[:], EPS)
            g.memset(onesf_neg[:], -1.0)
            i = g.memset(onesf_row[:], 1.0)
            sO.inc(i)
        ones_done = sO.n


        @bk.scalar
        def _(sc):
            sc.wait_ge(sW.h, prep_w)
            i = sc.activation(out=ktb[:], in_=ktb[:], func=AF.Abs, scale=0.13125)
            sA.inc(i)
        ktb_done = sA.n

        @bk.tensor
        def _(t):
            t.wait_ge(sA.h, ktb_done)
            t.wait_ge(sO.h, ones_done)
            t.matmul(pC[:, 0:1], onesf_row[:], ktb[:], start=True, stop=True)
            i = t.matmul(pC[0:2, 4:5], onesf_row[:, 0:2], ktraw[:], start=True, stop=True)
            sP.inc(i)
        ktp = sP.n

        @bk.vector
        def _(v):
            v.wait_ge(sP.h, ktp)
            v.tensor_scalar(out=negB[:], in0=pC[0:128, 0:1], scalar1=-1.0,
                            scalar2=None, op0=ALU.mult)
            i = v.tensor_copy(kt2[:], pC[0:2, 4:5])
            sV.inc(i)
        ktv = sV.n

        stA = contextlib.ExitStack()
        wkv_sb = stA.enter_context(nc.sbuf_tensor("wkv_sb", [128, NF * 512], BF16))
        stWQ = contextlib.ExitStack()
        wq_t2 = stWQ.enter_context(nc.sbuf_tensor("wq_t2", [128, 2 * LAT], BF16))
        qmtmp = stWQ.enter_context(nc.sbuf_tensor("qmtmp", [128, 64], F32))

        fc_pe_marks = {}
        for fc in range(NF):
            slot = fc % 2

            @bk.gpsimd
            def _(g, fc=fc, slot=slot, vn=sV.n,
                  pm=(fc_pe_marks.get(fc - 2, 0))):
                if fc >= 2:
                    g.wait_ge(sV.h, vn)
                    g.wait_ge(sP.h, pm)
                i = g.dma_start(out=wq_t2[:, slot * LAT:(slot + 1) * LAT],
                                in_=ap["wa_qT"][128 * fc:128 * (fc + 1), :])
                sG.inc(i)
                i = g.dma_start(out=wkv_sb[:, 512 * fc:512 * (fc + 1)],
                                in_=ap["wa_kvT"][128 * fc:128 * (fc + 1), :])
                sG.inc(i)
            gn = sG.n

            @bk.vector
            def _(v, fc=fc, slot=slot, gn=gn):
                v.wait_ge(sG.h, gn)
                wq = wq_t2[:, slot * LAT:(slot + 1) * LAT]
                src = wq.rearrange("p (h d) -> p d h", h=16)
                v.tensor_reduce(out=qmtmp[:], in_=src, axis=AX.X, op=ALU.add)
                v.tensor_scalar(out=qmT[:, 64 * fc:64 * (fc + 1)], in0=qmtmp[:],
                                scalar1=1.0 / 16, scalar2=None, op0=ALU.mult)
                srck = wkv_sb[:, 512 * fc:512 * fc + 256].rearrange("p (g d) -> p d g", g=4)
                v.tensor_reduce(out=qmtmp[:], in_=srck, axis=AX.X, op=ALU.add)
                i = v.tensor_scalar(out=kmT[:, 64 * fc:64 * (fc + 1)], in0=qmtmp[:],
                                    scalar1=0.25, scalar2=None, op0=ALU.mult)
                sV.inc(i)
            vn = sV.n

            @bk.tensor
            def _(t, fc=fc, slot=slot, gn=gn, vn=vn):
                t.wait_ge(sG.h, gn)
                t.wait_ge(sV.h, vn)
                st_, sp_ = (fc == 0), (fc == NF - 1)
                wq = wq_t2[:, slot * LAT:(slot + 1) * LAT]
                t.matmul(pB[0:1, 0:512], ones_col[:], wkv_sb[:, 512 * fc:512 * (fc + 1)],
                         start=st_, stop=sp_)
                t.matmul(pB[0:1, 512:576], ones_col[:], qmT[:, 64 * fc:64 * (fc + 1)],
                         start=st_, stop=sp_)
                t.matmul(pC[0:1, 0:512], ones_col[:], wq[:, 0:512], start=st_, stop=sp_)
                t.matmul(pA[0:1, 0:512], ones_col[:], wq[:, 512:1024], start=st_, stop=sp_)
                i = t.matmul(pA[0:1, 512:576], ones_col[:], kmT[:, 64 * fc:64 * (fc + 1)],
                             start=st_, stop=sp_)
                sP.inc(i)
            fc_pe_marks[fc] = sP.n
        w1_p = sP.n

        @bk.vector
        def _(v):
            v.wait_ge(sP.h, w1_p)
            v.tensor_copy(w1kv[:], pB[0:1, 0:512])
            v.tensor_copy(w1qm[:], pB[0:1, 512:576])
            v.tensor_copy(w1q[:, 0:512], pC[0:1, 0:512])
            v.tensor_copy(w1q[:, 512:1024], pA[0:1, 0:512])
            i = v.tensor_copy(w1km[:], pA[0:1, 512:576])
            sV.inc(i)
        barrier()
        stWQ.close()

        if "A" in phases:
            EA = stA.enter_context
            kcw = EA(nc.sbuf_tensor("kcw", [128, 24 * 128], BF16))
            kc1b_s = EA(nc.sbuf_tensor("kc1b_s", [128, 2], F32))
            kc2b_s = EA(nc.sbuf_tensor("kc2b_s", [128, 2], F32))
            xb_t = EA(nc.sbuf_tensor("xb_t", [128, 2 * NF * TW], BF16))
            sq_t = EA(nc.sbuf_tensor("sq_t", [128, 2 * 4 * TW], BF16))
            kvq = EA(nc.sbuf_tensor("kvq", [128, 4 * TW], BF16))
            qm_t = EA(nc.sbuf_tensor("qm_t", [64, TW], BF16))
            kpre = EA(nc.sbuf_tensor("kpre", [128, 2 * (TW + 2)], BF16))
            kc1o = EA(nc.sbuf_tensor("kc1o", [128, 2 * (TW + 2)], BF16))
            kf_t = EA(nc.sbuf_tensor("kf_t", [128, 2 * TW], F32))
            kfsq = EA(nc.sbuf_tensor("kfsq", [128, 2 * TW], BF16))
            rn_t = EA(nc.sbuf_tensor("rn_t", [2, 2 * TW], F32))
            krp = EA(nc.sbuf_tensor("krp", [128, 2 * TW], F32))
            ksw = EA(nc.sbuf_tensor("ksw", [128, TW], F32))
            cos_t = EA(nc.sbuf_tensor("cos_t", [128, TW], BF16))
            sin_t = EA(nc.sbuf_tensor("sin_t", [128, TW], BF16))
            vbuf = EA(nc.sbuf_tensor("vbuf", [128, TW + 1], BF16))
            vtr = EA(nc.sbuf_tensor("vtr", [128, 8 * 128], BF16))

            @bk.gpsimd
            def _(g):
                for tap in range(3):
                    for ic in range(2):
                        for oc in range(2):
                            off = (tap * 4 + ic * 2 + oc) * 128
                            i = g.dma_start(out=kcw[:, off:off + 128],
                                            in_=ap["kc1dT"][tap, 128 * ic:128 * (ic + 1),
                                                            128 * oc:128 * (oc + 1)])
                            sG.inc(i)
                            off2 = (12 + tap * 4 + ic * 2 + oc) * 128
                            i = g.dma_start(out=kcw[:, off2:off2 + 128],
                                            in_=ap["kc2dT"][tap, 128 * ic:128 * (ic + 1),
                                                            128 * oc:128 * (oc + 1)])
                            sG.inc(i)
            kcw_ld = sG.n

            @bk.sync
            def _(sy):
                sW.inc(sy.dma_start(out=kc1b_s[:], in_=ap["kc1b"]))
                sW.inc(sy.dma_start(out=kc2b_s[:], in_=ap["kc2b"]))
            kcb_ld = sW.n

            def kc1T(tap, ic, oc):
                o = (tap * 4 + ic * 2 + oc) * 128
                return kcw[:, o:o + 128]

            def kc2T(tap, ic, oc):
                o = (12 + tap * 4 + ic * 2 + oc) * 128
                return kcw[:, o:o + 128]

            rope_done = 0
            tile_pe_done = 0
            k_store = 0
            v_store = 0
            sqrt_prev = 0
            krp_prev = 0
            kv2_done = {}
            qblk_marks = {}
            for tt in range(NT):
                c0 = TW * tt
                bstart = (tt % 4 == 0)
                slot = tt % 2
                xbv = xb_t[:, slot * NF * TW:(slot + 1) * NF * TW]

                sqo_marks = {}

                @bk.gpsimd
                def _(g, tt=tt, c0=c0, xbv=xbv, pe=kv2_done.get(tt - 2, 0),
                      qm=[qblk_marks.get(tt * 8 + b - 2, 0) for b in range(8)],
                      sqm=sqo_marks):
                    if pe:
                        g.wait_ge(sP.h, pe)
                    src = ap["x_t"].rearrange("(c p) t -> p c t", p=128)
                    i = g.dma_start(out=xbv.rearrange("p (c w) -> p c w", c=NF),
                                    in_=src[:, :, c0:c0 + TW])
                    sG.inc(i)
                    g.wait_ge(sG.h, sG.n)
                    for b in range(8):
                        if qm[b]:
                            g.wait_ge(sP.h, qm[b])
                        xs = (xbv.rearrange("p (c w) -> p c w", c=NF)
                              [:, 4 * b:4 * b + 4, :])
                        ds = (sq_t[:, (b % 2) * 2048:(b % 2) * 2048 + 2048]
                              .rearrange("p (c w) -> p c w", c=4))
                        i = g.tensor_mul(ds, xs, xs)
                        sO.inc(i)
                        sqm[b] = sO.n
                    i = g.dma_start(out=ap["wg_bf"][256 * tt:256 * (tt + 1), :],
                                    in_=ap["wgT"][256 * tt:256 * (tt + 1), :])
                    sG.inc(i)
                    i = g.dma_start(out=ap["wu_bf"][256 * tt:256 * (tt + 1), :],
                                    in_=ap["wuT"][256 * tt:256 * (tt + 1), :])
                    sG.inc(i)
                    i = g.dma_start(out=ap["wd_bf"][688 * tt:688 * (tt + 1), :],
                                    in_=ap["wdT"][688 * tt:688 * (tt + 1), :])
                    sG.inc(i)
                xb_ready = sG.n - 48

                @bk.gpsimd
                def _(g, c0=c0, rd=rope_done):
                    if rd:
                        g.wait_ge(sV.h, rd)
                    sG.inc(g.dma_start(out=cos_t[:], in_=ap["cos2"][:, c0:c0 + TW]))
                    sG.inc(g.dma_start(out=sin_t[:], in_=ap["sin2"][:, c0:c0 + TW]))
                trig_ready = sG.n

                @bk.tensor
                def _(t, tt=tt, xbv=xbv, xbr=xb_ready, sqr=dict(sqo_marks),
                      pa=sqrt_prev, pv=krp_prev):
                    t.wait_ge(sG.h, xbr)
                    if tt == 0:
                        t.wait_ge(sG.h, kcw_ld)
                        t.wait_ge(sW.h, kcb_ld)
                    if pv:
                        t.wait_ge(sV.h, pv)
                    if pa:
                        t.wait_ge(sA.h, pa)
                    for fc in range(NF):
                        i = t.matmul(pC[0:1, 0:512], ones_col[:],
                                     xbv[:, TW * fc:TW * (fc + 1)],
                                     start=(fc == 0), stop=(fc == NF - 1))
                    sP.inc(i)
                    for b in range(8):
                        t.wait_ge(sO.h, sqr[b])
                        for f4 in range(4):
                            i = t.matmul(pB[0:1, 0:512], ones_col[:],
                                         sq_t[:, (b % 2) * 2048 + f4 * 512:
                                              (b % 2) * 2048 + f4 * 512 + 512],
                                         start=(b == 0 and f4 == 0),
                                         stop=(b == 7 and f4 == 3))
                        i.then_inc(sP.h, 1)
                        sP.n += 1
                        qblk_marks[tt * 8 + b] = sP.n
                    for ot in range(4):
                        for fc in range(NF):
                            t.matmul(pA[:, 512 * ot:512 * ot + 512],
                                     wkv_sb[:, 512 * fc + 128 * ot:512 * fc + 128 * (ot + 1)],
                                     xbv[:, TW * fc:TW * (fc + 1)],
                                     start=(fc == 0), stop=False)
                s_done = sP.n

                stats_done = stats_math(pC[0:1, 0:TW], pB[0:1, 0:TW], TW, D)

                @bk.tensor
                def _(t, sd=stats_done, xbv=xbv):
                    t.wait_ge(sV.h, sd)
                    t.matmul(pB[:, 512:1024], onesf_row[:], arow[:, 0:TW],
                             start=True, stop=True)
                    for fc in range(NF):
                        t.matmul(pB[0:64, 0:512], qmT[:, 64 * fc:64 * (fc + 1)],
                                 xbv[:, TW * fc:TW * (fc + 1)],
                                 start=(fc == 0), stop=False)
                    for ot in range(4):
                        t.matmul(pA[:, 512 * ot:512 * ot + 512],
                                 w1kv[:, 128 * ot:128 * (ot + 1)], crow[:, 0:TW],
                                 start=False, stop=True)
                    i = t.matmul(pB[0:64, 0:512], w1qm[:], crow[:, 0:TW],
                                 start=False, stop=True)
                    sP.inc(i)
                kv_done = sP.n
                tile_pe_done = kv_done
                kv2_done[tt] = kv_done

                @bk.vector
                def _(v, kvd=kv_done, tt=tt, bstart=bstart):
                    v.wait_ge(sP.h, kvd)
                    v.tensor_copy(a_b[:], pB[:, 512:1024])
                    for ot in range(4):
                        v.tensor_tensor(out=kvq[:, 512 * ot:512 * (ot + 1)],
                                        in0=pA[:, 512 * ot:512 * ot + 512],
                                        in1=a_b[:], op=ALU.mult)
                    v.tensor_tensor(out=qm_t[:], in0=pB[0:64, 0:512],
                                    in1=a_b[0:64, :], op=ALU.mult)
                    for c in range(2):
                        if bstart:
                            v.memset(kpre[:, c * 514:c * 514 + 2], 0.0)
                            v.memset(kc1o[:, c * 514:c * 514 + 2], 0.0)
                        else:
                            v.tensor_copy(kpre[:, c * 514:c * 514 + 2],
                                          kpre[:, c * 514 + 512:c * 514 + 514])
                            v.tensor_copy(kc1o[:, c * 514:c * 514 + 2],
                                          kc1o[:, c * 514 + 512:c * 514 + 514])
                    if bstart:
                        v.memset(vbuf[:, 0:1], 0.0)
                    else:
                        v.tensor_copy(vbuf[:, 0:1], vbuf[:, TW:TW + 1])
                    for c in range(2):
                        v.tensor_copy(kpre[:, c * 514 + 2:c * 514 + 514],
                                      kvq[:, 512 * c:512 * (c + 1)])
                    i = v.tensor_copy(vbuf[:, 1:TW + 1], kvq[:, 512 * 3:512 * 4])
                    sV.inc(i)
                kvq_done = sV.n

                @bk.tensor
                def _(t, kd=kvq_done):
                    t.wait_ge(sV.h, kd)
                    for oc in range(2):
                        first = True
                        for tap in range(3):
                            for ic in range(2):
                                i = t.matmul(pA[:, 512 * oc:512 * oc + 512],
                                             kc1T(tap, ic, oc),
                                             kpre[:, ic * 514 + tap:ic * 514 + tap + 512],
                                             start=first, stop=(tap == 2 and ic == 1))
                                first = False
                    sP.inc(i)
                kc1_done = sP.n

                @bk.scalar
                def _(sc, p=kc1_done, kd=kvq_done):
                    sc.wait_ge(sP.h, p)
                    sc.wait_ge(sV.h, kd)
                    for oc in range(2):
                        i = sc.activation(out=kc1o[:, oc * 514 + 2:oc * 514 + 514],
                                          in_=pA[:, 512 * oc:512 * oc + 512],
                                          func=AF.Identity, bias=kc1b_s[:, oc:oc + 1])
                    sA.inc(i)
                kc1o_done = sA.n

                @bk.tensor
                def _(t, a1=kc1o_done, kd=kvq_done):
                    t.wait_ge(sA.h, a1)
                    for oc in range(2):
                        first = True
                        for tap in range(3):
                            for ic in range(2):
                                i = t.matmul(pA[:, 1024 + 512 * oc:1024 + 512 * oc + 512],
                                             kc2T(tap, ic, oc),
                                             kc1o[:, ic * 514 + tap:ic * 514 + tap + 512],
                                             start=first, stop=(tap == 2 and ic == 1))
                                first = False
                    i = t.matmul(pB[:, 512:1024], sel64_s[:], qm_t[:],
                                 start=True, stop=True)
                    sP.inc(i)
                kc2_done = sP.n

                @bk.scalar
                def _(sc, p=kc2_done):
                    sc.wait_ge(sP.h, p)
                    for oc in range(2):
                        i = sc.activation(out=kf_t[:, TW * oc:TW * (oc + 1)],
                                          in_=pA[:, 1024 + 512 * oc:1024 + 512 * oc + 512],
                                          func=AF.Identity, bias=kc2b_s[:, oc:oc + 1])
                    sA.inc(i)
                kc2o_done = sA.n

                @bk.vector
                def _(v, a1=kc2o_done):
                    v.wait_ge(sA.h, a1)
                    for oc in range(2):
                        i = v.scalar_tensor_tensor(out=kf_t[:, TW * oc:TW * (oc + 1)],
                                                   in0=pB[:, 512:1024], scalar=0.5,
                                                   in1=kf_t[:, TW * oc:TW * (oc + 1)],
                                                   op0=ALU.mult, op1=ALU.add)
                    sV.inc(i)
                kf_done = sV.n

                @bk.scalar
                def _(sc, vv=kf_done):
                    sc.wait_ge(sV.h, vv)
                    i = sc.activation(out=kfsq[:], in_=kf_t[:], func=AF.Square)
                    sA.inc(i)
                kfsq_done = sA.n

                @bk.tensor
                def _(t, a1=kfsq_done):
                    t.wait_ge(sA.h, a1)
                    t.matmul(pC[0:2, 0:512], bdo_s[:], kfsq[:, 0:TW], start=True, stop=True)
                    i = t.matmul(pB[0:2, 0:512], bdo_s[:], kfsq[:, TW:2 * TW],
                                 start=True, stop=True)
                    sP.inc(i)
                n2_done = sP.n

                @bk.scalar
                def _(sc, p=n2_done):
                    sc.wait_ge(sP.h, p)
                    sc.activation(out=rn_t[:, 0:TW], in_=pC[0:2, 0:512], func=AF.Sqrt)
                    i = sc.activation(out=rn_t[:, TW:2 * TW], in_=pB[0:2, 0:512], func=AF.Sqrt)
                    sA.inc(i)
                sqrt_done = sA.n

                @bk.vector
                def _(v, a1=sqrt_done):
                    v.wait_ge(sA.h, a1)
                    v.tensor_scalar(out=rn_t[:], in0=rn_t[:], scalar1=1e-12,
                                    scalar2=None, op0=ALU.max)
                    v.reciprocal(rn_t[:], rn_t[:])
                    i = v.tensor_scalar(out=rn_t[:], in0=rn_t[:], scalar1=kt2[:],
                                        scalar2=None, op0=ALU.mult)
                    sV.inc(i)
                rn_done = sV.n

                @bk.tensor
                def _(t, vv=rn_done):
                    t.wait_ge(sV.h, vv)
                    t.matmul(pB[:, 512:1024], selkf_s[:], rn_t[:, 0:TW],
                             start=True, stop=True)
                    i = t.matmul(pB[:, 0:512], selkf_s[:], rn_t[:, TW:2 * TW],
                                 start=True, stop=True)
                    sP.inc(i)
                rnb_done = sP.n

                @bk.vector
                def _(v, p=rnb_done, ks=k_store):
                    v.wait_ge(sP.h, p)
                    if ks:
                        v.wait_ge(sS.h, ks)
                    v.tensor_tensor(out=krp[:, 0:TW], in0=kf_t[:, 0:TW],
                                    in1=pB[:, 512:1024], op=ALU.mult)
                    i = v.tensor_tensor(out=krp[:, TW:2 * TW], in0=kf_t[:, TW:2 * TW],
                                        in1=pB[:, 0:512], op=ALU.mult)
                    sV.inc(i)
                krp_prev = sV.n
                sqrt_prev = sA.n

                @bk.vector
                def _(v, tr=trig_ready):
                    v.wait_ge(sG.h, tr)
                    for c in range(2):
                        a0 = c * TW
                        for blk in range(2):
                            r0, r1 = 64 * blk, 64 * blk + 32
                            v.tensor_copy(ksw[r0:r0 + 32, :], krp[r1:r1 + 32, a0:a0 + TW])
                            v.tensor_copy(ksw[r1:r1 + 32, :], krp[r0:r0 + 32, a0:a0 + TW])
                        v.tensor_tensor(out=krp[:, a0:a0 + TW], in0=krp[:, a0:a0 + TW],
                                        in1=cos_t[:], op=ALU.mult)
                        v.tensor_tensor(out=ksw[:], in0=ksw[:], in1=sin_t[:], op=ALU.mult)
                        i = v.tensor_tensor(out=krp[:, a0:a0 + TW], in0=krp[:, a0:a0 + TW],
                                            in1=ksw[:], op=ALU.add)
                    sV.inc(i)
                rope_done = sV.n

                @bk.gpsimd
                def _(g, vv=rope_done, c0=c0):
                    g.wait_ge(sV.h, vv)
                    for c in range(2):
                        i = g.dma_start(out=ap["K_dram"][128 * c:128 * (c + 1), c0:c0 + TW],
                                        in_=krp[:, TW * c:TW * (c + 1)])
                        sS.inc(i)
                k_store = sS.n

                @bk.tensor
                def _(t, kd=kvq_done, rnb=rnb_done):
                    t.wait_ge(sV.h, kd)
                    for i_ in range(4):
                        t.transpose(pD[:, 128 * i_:128 * (i_ + 1)],
                                    kvq[:, 1024 + 128 * i_:1024 + 128 * (i_ + 1)], eye_s[:])
                    for i_ in range(4):
                        i = t.transpose(pD[:, 512 + 128 * i_:512 + 128 * (i_ + 1)],
                                        vbuf[:, 128 * i_:128 * i_ + 128], eye_s[:])
                    sP.inc(i)
                vtr_done = sP.n

                @bk.vector
                def _(v, p=vtr_done, vs=v_store):
                    v.wait_ge(sP.h, p)
                    if vs:
                        v.wait_ge(sS.h, vs)
                    i = v.tensor_copy(vtr[:], pD[:])
                    sV.inc(i)
                vtr_evac = sV.n

                @bk.sync
                def _(sy, vv=vtr_evac, c0=c0):
                    sy.wait_ge(sV.h, vv)
                    for c in range(2):
                        for i_ in range(4):
                            i = sy.dma_start(
                                out=ap["V_dram"][c0 + 128 * i_:c0 + 128 * (i_ + 1),
                                                 128 * c:128 * (c + 1)],
                                in_=vtr[:, (c * 4 + i_) * 128:(c * 4 + i_ + 1) * 128])
                            sS.inc(i)
                v_store = sS.n

            barrier()
        stA.close()

        def waitall(e, snap):
            for h, n in snap:
                if n:
                    e.wait_ge(h, n)

        def snap():
            return [(c.h, c.n) for c in CHANS]

        if "B" in phases:
            stB = contextlib.ExitStack()
            EB = stB.enter_context
            sqB = EB(nc.sbuf_tensor("sqB", [128, 4 * 512], BF16))
            qsq = EB(nc.sbuf_tensor("qsq", [128, 1024], BF16))
            qpre = EB(nc.sbuf_tensor("qpre", [128, 8 * 1040], BF16))
            km_t = EB(nc.sbuf_tensor("km_t", [64, 1040], BF16))
            qc1o = EB(nc.sbuf_tensor("qc1o", [128, 8 * 1040], BF16))
            a_bB = EB(nc.sbuf_tensor("a_bB", [128, 3 * 512], F32))
            crowB = EB(nc.sbuf_tensor("crowB", [1, 3 * 512], F32))
            wqB = EB(nc.sbuf_tensor("wqB", [128, NF * 128], BF16))
            qcw = EB(nc.sbuf_tensor("qcw", [128, 24 * 128], BF16))
            qc1b_s = EB(nc.sbuf_tensor("qc1b_s", [128, 8], F32))
            qc2b_s = EB(nc.sbuf_tensor("qc2b_s", [128, 8], F32))
            qtmp = EB(nc.sbuf_tensor("qtmp", [128, 256], F32))
            rnq = EB(nc.sbuf_tensor("rnq", [2, 1024], F32))
            qswp = EB(nc.sbuf_tensor("qswp", [128, 1024], F32))
            cosq = EB(nc.sbuf_tensor("cosq", [128, 1024], F32))
            sinq = EB(nc.sbuf_tensor("sinq", [128, 1024], F32))
            stB1 = contextlib.ExitStack()
            xoh = stB1.enter_context(nc.sbuf_tensor("xoh", [128, NF * 1040], BF16))

            @bk.gpsimd
            def _(g, sn=snap()):
                waitall(g, sn)
                src = ap["x_ownh"].rearrange("(c p) t -> p c t", p=128)
                i = g.dma_start(out=xoh[:].rearrange("p (c w) -> p c w", c=NF), in_=src)
                sG.inc(i)

            @bk.sync
            def _(sy, sn=snap()):
                waitall(sy, sn)
                sW.inc(sy.dma_start(out=qc1b_s[:], in_=ap["qc1b"]))
                sW.inc(sy.dma_start(out=qc2b_s[:], in_=ap["qc2b"]))
                sW.inc(sy.dma_start(out=cosq[:], in_=ap["cos2o"]))
                sW.inc(sy.dma_start(out=sinq[:], in_=ap["sin2o"]))

            CG = [(0, 512), (512, 512), (1024, 16)]
            for icg, (s0, w) in enumerate(CG):
                @bk.tensor
                def _(t, s0=s0, w=w, sn=snap()):
                    waitall(t, sn)
                    for fc in range(NF):
                        i = t.matmul(pC[0:1, 0:w], ones_col[:],
                                     xoh[:, fc * 1040 + s0:fc * 1040 + s0 + w],
                                     start=(fc == 0), stop=(fc == NF - 1))
                    sP.inc(i)

                for blk in range(8):
                    @bk.gpsimd
                    def _(g, s0=s0, w=w, blk=blk, sn=snap()):
                        waitall(g, sn)
                        src = (xoh[:].rearrange("p (c t) -> p c t", c=NF)
                               [:, 4 * blk:4 * blk + 4, s0:s0 + w])
                        dst = sqB[:].rearrange("p (c t) -> p c t", c=4)[:, :, 0:w]
                        i = g.tensor_mul(dst, src, src)
                        sO.inc(i)

                    @bk.tensor
                    def _(t, w=w, blk=blk, sn=snap()):
                        waitall(t, sn)
                        for f4 in range(4):
                            i = t.matmul(pB[0:1, 0:w], ones_col[:],
                                         sqB[:, f4 * 512:f4 * 512 + w],
                                         start=(blk == 0 and f4 == 0),
                                         stop=(blk == 7 and f4 == 3))
                        sP.inc(i)

                sd = stats_math(pC[0:1, 0:w], pB[0:1, 0:w], w, D)

                @bk.tensor
                def _(t, w=w, sd=sd):
                    t.wait_ge(sV.h, sd)
                    i = t.matmul(pB[:, 512:512 + w], onesf_row[:], arow[:, 0:w],
                                 start=True, stop=True)
                    sP.inc(i)

                @bk.vector
                def _(v, icg=icg, w=w, sn=snap()):
                    waitall(v, sn)
                    v.tensor_copy(a_bB[:, icg * 512:icg * 512 + w], pB[:, 512:512 + w])
                    i = v.tensor_copy(crowB[:, icg * 512:icg * 512 + w], crow[:, 0:w])
                    sV.inc(i)

            for ot in range(9):
                slot = ot % 2

                @bk.gpsimd
                def _(g, ot=ot, sn=snap()):
                    waitall(g, sn)
                    if ot < 8:
                        src = ap["wa_qT"].rearrange("(c p) o -> p c o", p=128)
                        i = g.dma_start(
                            out=wqB[:].rearrange("p (c o) -> p c o", c=NF),
                            in_=src[:, :, 128 * ot:128 * (ot + 1)])
                        sG.inc(i)

                for icg, (s0, w) in enumerate(CG):
                    @bk.tensor
                    def _(t, ot=ot, icg=icg, s0=s0, w=w, sn=snap()):
                        waitall(t, sn)
                        bank = pA[:, 1024 * (icg % 2):1024 * (icg % 2) + w]
                        for fc in range(NF):
                            lhs = (wqB[:, fc * 128:fc * 128 + 128]
                                   if ot < 8 else kmT[:, 64 * fc:64 * (fc + 1)])
                            t.matmul(bank[0:128 if ot < 8 else 64, :], lhs,
                                     xoh[:, fc * 1040 + s0:fc * 1040 + s0 + w],
                                     start=(fc == 0), stop=False)
                        w1s = (w1q[:, 128 * ot:128 * (ot + 1)] if ot < 8 else w1km[:])
                        i = t.matmul(bank[0:128 if ot < 8 else 64, :], w1s,
                                     crowB[:, icg * 512:icg * 512 + w],
                                     start=False, stop=True)
                        sP.inc(i)

                    @bk.vector
                    def _(v, ot=ot, icg=icg, s0=s0, w=w, sn=snap()):
                        waitall(v, sn)
                        bank = pA[:, 1024 * (icg % 2):1024 * (icg % 2) + w]
                        ab = a_bB[:, icg * 512:icg * 512 + w]
                        if ot < 8:
                            i = v.tensor_tensor(out=qpre[:, ot * 1040 + s0:ot * 1040 + s0 + w],
                                                in0=bank, in1=ab, op=ALU.mult)
                        else:
                            i = v.tensor_tensor(out=km_t[:, s0:s0 + w],
                                                in0=bank[0:64, :], in1=ab[0:64, :],
                                                op=ALU.mult)
                        sV.inc(i)

            barrier()
            stB1.close()
            qf = EB(nc.sbuf_tensor("qf", [128, 8 * 1024], F32))

            for oc in range(8):
                slot = oc % 2

                @bk.gpsimd
                def _(g, oc=oc, sn=snap()):
                    waitall(g, sn)
                    for tap in range(3):
                        src = ap["qc1dT"][tap].rearrange("(c p) o -> p c o", p=128)
                        i = g.dma_start(
                            out=qcw[:, (tap * 8) * 128:(tap * 8 + 8) * 128]
                                .rearrange("p (c o) -> p c o", c=8),
                            in_=src[:, :, 128 * oc:128 * (oc + 1)])
                        sG.inc(i)

                for b_ in range(4):
                    @bk.tensor
                    def _(t, oc=oc, b_=b_, sn=snap()):
                        waitall(t, sn)
                        bank = pA[:, 1024 * (b_ % 2):1024 * (b_ % 2) + 258]
                        first = True
                        for tap in range(3):
                            for ic in range(8):
                                lhs = qcw[:, (tap * 8 + ic) * 128:
                                          (tap * 8 + ic + 1) * 128]
                                i = t.matmul(bank, lhs,
                                             qpre[:, ic * 1040 + b_ * 260 + tap:
                                                  ic * 1040 + b_ * 260 + tap + 258],
                                             start=first, stop=(tap == 2 and ic == 7))
                                first = False
                        sP.inc(i)

                    @bk.scalar
                    def _(sc, oc=oc, b_=b_, sn=snap()):
                        waitall(sc, sn)
                        bank = pA[:, 1024 * (b_ % 2):1024 * (b_ % 2) + 258]
                        i = sc.activation(out=qc1o[:, oc * 1040 + b_ * 260 + 2:
                                                   oc * 1040 + b_ * 260 + 260],
                                          in_=bank, func=AF.Identity,
                                          bias=qc1b_s[:, oc:oc + 1])
                        sA.inc(i)

            for b_ in range(4):
                @bk.tensor
                def _(t, b_=b_, sn=snap()):
                    waitall(t, sn)
                    i = t.matmul(pB[:, 512:768], sel64_s[:],
                                 km_t[:, b_ * 260 + 4:b_ * 260 + 260],
                                 start=True, stop=True)
                    sP.inc(i)
                kmb_p = sP.n

                for oc in range(8):
                    slot = oc % 2

                    @bk.gpsimd
                    def _(g, oc=oc, sn=snap()):
                        waitall(g, sn)
                        for tap in range(3):
                            src = ap["qc2dT"][tap].rearrange("(c p) o -> p c o", p=128)
                            i = g.dma_start(
                                out=qcw[:, (tap * 8) * 128:(tap * 8 + 8) * 128]
                                    .rearrange("p (c o) -> p c o", c=8),
                                in_=src[:, :, 128 * oc:128 * (oc + 1)])
                            sG.inc(i)

                    @bk.tensor
                    def _(t, oc=oc, b_=b_, sn=snap()):
                        waitall(t, sn)
                        bank = pA[:, 1024 * (oc % 2):1024 * (oc % 2) + 256]
                        first = True
                        for tap in range(3):
                            for ic in range(8):
                                lhs = qcw[:, (tap * 8 + ic) * 128:
                                          (tap * 8 + ic + 1) * 128]
                                i = t.matmul(bank, lhs,
                                             qc1o[:, ic * 1040 + b_ * 260 + 2 + tap:
                                                  ic * 1040 + b_ * 260 + tap + 258],
                                             start=first, stop=(tap == 2 and ic == 7))
                                first = False
                        sP.inc(i)

                    @bk.scalar
                    def _(sc, oc=oc, b_=b_, sn=snap()):
                        waitall(sc, sn)
                        bank = pA[:, 1024 * (oc % 2):1024 * (oc % 2) + 256]
                        i = sc.activation(out=qtmp[:], in_=bank, func=AF.Identity,
                                          bias=qc2b_s[:, oc:oc + 1])
                        sA.inc(i)

                    @bk.vector
                    def _(v, oc=oc, b_=b_, sn=snap(), kp=kmb_p):
                        waitall(v, sn)
                        v.wait_ge(sP.h, kp)
                        i = v.scalar_tensor_tensor(
                            out=qf[:, oc * 1024 + b_ * 256:oc * 1024 + b_ * 256 + 256],
                            in0=pB[:, 512:768], scalar=0.5, in1=qtmp[:],
                            op0=ALU.mult, op1=ALU.add)
                        sV.inc(i)

            for oc in range(8):
                @bk.scalar
                def _(sc, oc=oc, sn=snap()):
                    waitall(sc, sn)
                    i = sc.activation(out=qsq[:], in_=qf[:, oc * 1024:(oc + 1) * 1024],
                                      func=AF.Square)
                    sA.inc(i)

                @bk.tensor
                def _(t, sn=snap()):
                    waitall(t, sn)
                    t.matmul(pC[0:2, 0:512], bdo_s[:], qsq[:, 0:512], start=True, stop=True)
                    i = t.matmul(pB[0:2, 0:512], bdo_s[:], qsq[:, 512:1024],
                                 start=True, stop=True)
                    sP.inc(i)

                @bk.scalar
                def _(sc, sn=snap()):
                    waitall(sc, sn)
                    sc.activation(out=rnq[:, 0:512], in_=pC[0:2, 0:512], func=AF.Sqrt)
                    i = sc.activation(out=rnq[:, 512:1024], in_=pB[0:2, 0:512], func=AF.Sqrt)
                    sA.inc(i)

                @bk.vector
                def _(v, sn=snap()):
                    waitall(v, sn)
                    v.tensor_scalar(out=rnq[:], in0=rnq[:], scalar1=1e-12,
                                    scalar2=None, op0=ALU.max)
                    i = v.reciprocal(rnq[:], rnq[:])
                    sV.inc(i)

                @bk.tensor
                def _(t, sn=snap()):
                    waitall(t, sn)
                    t.matmul(pB[:, 512:1024], selkf_s[:], rnq[:, 0:512], start=True, stop=True)
                    i = t.matmul(pC[:, 0:512], selkf_s[:], rnq[:, 512:1024],
                                 start=True, stop=True)
                    sP.inc(i)

                @bk.vector
                def _(v, oc=oc, sn=snap()):
                    waitall(v, sn)
                    v.tensor_tensor(out=qf[:, oc * 1024:oc * 1024 + 512],
                                    in0=qf[:, oc * 1024:oc * 1024 + 512],
                                    in1=pB[:, 512:1024], op=ALU.mult)
                    v.tensor_tensor(out=qf[:, oc * 1024 + 512:oc * 1024 + 1024],
                                    in0=qf[:, oc * 1024 + 512:oc * 1024 + 1024],
                                    in1=pC[:, 0:512], op=ALU.mult)
                    qslice = qf[:, oc * 1024:(oc + 1) * 1024]
                    for blk in range(2):
                        r0, r1 = 64 * blk, 64 * blk + 32
                        v.tensor_copy(qswp[r0:r0 + 32, :], qslice[r1:r1 + 32, :])
                        v.tensor_copy(qswp[r1:r1 + 32, :], qslice[r0:r0 + 32, :])
                    v.tensor_tensor(out=qslice, in0=qslice, in1=cosq[:], op=ALU.mult)
                    v.tensor_tensor(out=qswp[:], in0=qswp[:], in1=sinq[:], op=ALU.mult)
                    i = v.tensor_tensor(out=Q[:, oc * 1024:(oc + 1) * 1024],
                                        in0=qslice, in1=qswp[:], op=ALU.add)
                    sV.inc(i)

            barrier()
            stB.close()
        stQM.close()

        if "C" in phases:
            stC = contextlib.ExitStack()
            EC = stC.enter_context
            AO = EC(nc.sbuf_tensor("AO", [128, 8 * 1024], BF16))
            PT = EC(nc.sbuf_tensor("PT", [128, 16 * 256], BF16))
            mask_s = EC(nc.sbuf_tensor("mask_s", [128, 16 * 256], BF16))
            ksl = EC(nc.sbuf_tensor("ksl", [128, 2048], BF16))
            vsl = EC(nc.sbuf_tensor("vsl", [128, 16 * 64], BF16))
            rlrow = EC(nc.sbuf_tensor("rlrow", [1, 256], F32))
            ao_t = EC(nc.sbuf_tensor("ao_t", [64, 256], F32))
            woB = EC(nc.sbuf_tensor("woB", [128, 2 * 8 * 128], BF16))
            xrB = EC(nc.sbuf_tensor("xrB", [128, 2 * 512], F32))
            xoB = EC(nc.sbuf_tensor("xoB", [128, 2 * 512], F32))

            @bk.sync
            def _(sy, sn=snap()):
                waitall(sy, sn)
                sW.inc(sy.dma_start(out=mask_s[:], in_=ap["mask"]))

            for b_ in range(B):
                for g_ in range(HKV):
                    @bk.sync
                    def _(sy, b_=b_, g_=g_, sn=snap()):
                        waitall(sy, sn)
                        r0 = 128 * (g_ // 2) + 64 * (g_ % 2)
                        i = sy.dma_start(out=ksl[0:64, :],
                                         in_=ap["K_dram"][r0:r0 + 64,
                                                          b_ * 2048:(b_ + 1) * 2048])
                        sW.inc(i)
                        i = sy.dma_start(out=ksl[64:128, :],
                                         in_=ap["K_dram"][r0:r0 + 64,
                                                          b_ * 2048:(b_ + 1) * 2048])
                        sW.inc(i)
                        src = ap["V_dram"].rearrange("(i p) d -> p i d", p=128)
                        i = sy.dma_start(out=vsl[:].rearrange("p (i d) -> p i d", i=16),
                                         in_=src[:, 16 * b_:16 * (b_ + 1),
                                                 64 * g_:64 * (g_ + 1)])
                        sW.inc(i)

                    for hh in range(4):
                        h = 4 * g_ + hh
                        qrhs = Q[64 * (h % 2):64 * (h % 2) + 64,
                                 (h // 2) * 1024 + b_ * 256:(h // 2) * 1024 + b_ * 256 + 256]

                        for wv in range(2):
                            @bk.tensor
                            def _(t, wv=wv, qrhs=qrhs, wv_h=h, sn=snap()):
                                waitall(t, sn)
                                hb = 64 * (wv_h % 2)
                                for i_ in range(8):
                                    tk = 8 * wv + i_
                                    i = t.matmul(pA[:, 256 * i_:256 * (i_ + 1)],
                                                 ksl[hb:hb + 64, 128 * tk:128 * (tk + 1)],
                                                 qrhs, start=True, stop=True)
                                sP.inc(i)

                            @bk.scalar
                            def _(sc, wv=wv, sn=snap()):
                                waitall(sc, sn)
                                i = sc.activation(out=PT[:, 2048 * wv:2048 * (wv + 1)],
                                                  in_=pA[:, 0:2048], func=AF.Exp,
                                                  scale=0.125, bias=negB[:])
                                sA.inc(i)

                        @bk.vector
                        def _(v, sn=snap()):
                            waitall(v, sn)
                            i = v.tensor_tensor(out=PT[:], in0=PT[:], in1=mask_s[:],
                                                op=ALU.mult)
                            sV.inc(i)

                        @bk.tensor
                        def _(t, sn=snap()):
                            waitall(t, sn)
                            for tk in range(16):
                                i = t.matmul(pC[0:1, 0:256], ones_col[:],
                                             PT[:, 256 * tk:256 * (tk + 1)],
                                             start=(tk == 0), stop=(tk == 15))
                            sP.inc(i)
                            for tk in range(16):
                                i = t.matmul(pA[0:64, 0:256],
                                             vsl[:, 64 * tk:64 * (tk + 1)],
                                             PT[:, 256 * tk:256 * (tk + 1)],
                                             start=(tk == 0), stop=(tk == 15))
                            sP.inc(i)

                        @bk.vector
                        def _(v, sn=snap()):
                            waitall(v, sn)
                            i = v.reciprocal(rlrow[:], pC[0:1, 0:256])
                            sV.inc(i)

                        @bk.tensor
                        def _(t, sn=snap()):
                            waitall(t, sn)
                            i = t.matmul(pB[0:64, 0:256], onesf_row[:, 0:64], rlrow[:],
                                         start=True, stop=True)
                            sP.inc(i)

                        @bk.vector
                        def _(v, h=h, b_=b_, sn=snap()):
                            waitall(v, sn)
                            v.tensor_copy(ao_t[:], pA[0:64, 0:256])
                            dst = AO[64 * (h % 2):64 * (h % 2) + 64,
                                     (h // 2) * 1024 + b_ * 256:(h // 2) * 1024 + b_ * 256 + 256]
                            i = v.tensor_tensor(out=dst, in0=ao_t[:], in1=pB[0:64, 0:256],
                                                op=ALU.mult)
                            sV.inc(i)

            for oc in range(NF):
                slot = oc % 2

                @bk.gpsimd
                def _(g, oc=oc, slot=slot, sn=snap()):
                    waitall(g, sn)
                    src = ap["wa_oT"].rearrange("(c p) o -> p c o", p=128)
                    i = g.dma_start(
                        out=woB[:, slot * 1024:(slot + 1) * 1024]
                            .rearrange("p (c o) -> p c o", c=8),
                        in_=src[:, :, 128 * oc:128 * (oc + 1)])
                    sG.inc(i)

                @bk.sync
                def _(sy, oc=oc, slot=slot, sn=snap()):
                    waitall(sy, sn)
                    i = sy.dma_start(out=xrB[:, slot * 512:slot * 512 + 512],
                                     in_=ap["x_own"][128 * oc:128 * (oc + 1), 0:512])
                    sW.inc(i)
                    i = sy.dma_start(out=xoB[:, slot * 512:slot * 512 + 512],
                                     in_=ap["x_own"][128 * oc:128 * (oc + 1), 512:1024])
                    sW.inc(i)

                for cg in range(2):
                    @bk.tensor
                    def _(t, oc=oc, slot=slot, cg=cg, sn=snap()):
                        waitall(t, sn)
                        bank = pA[:, 1024 * cg:1024 * cg + 512]
                        for lc in range(8):
                            i = t.matmul(bank,
                                         woB[:, slot * 1024 + lc * 128:slot * 1024 + lc * 128 + 128],
                                         AO[:, lc * 1024 + 512 * cg:lc * 1024 + 512 * cg + 512],
                                         start=(lc == 0), stop=(lc == 7))
                        sP.inc(i)

                    @bk.vector
                    def _(v, oc=oc, slot=slot, cg=cg, sn=snap()):
                        waitall(v, sn)
                        bank = pA[:, 1024 * cg:1024 * cg + 512]
                        res = (xrB if cg == 0 else xoB)[:, slot * 512:slot * 512 + 512]
                        i = v.tensor_tensor(out=(xrB if cg == 0 else xoB)
                                            [:, slot * 512:slot * 512 + 512],
                                            in0=bank, in1=res, op=ALU.add)
                        sV.inc(i)

                    @bk.sync
                    def _(sy, oc=oc, slot=slot, cg=cg, sn=snap()):
                        waitall(sy, sn)
                        i = sy.dma_start(out=ap["x1_dram"][128 * oc:128 * (oc + 1),
                                                           512 * cg:512 * cg + 512],
                                         in_=(xrB if cg == 0 else xoB)
                                         [:, slot * 512:slot * 512 + 512])
                        sS.inc(i)

            barrier()
            stC.close()
        stBC.close()

        def ffn_phase(xin, xout):
            stD = contextlib.ExitStack()
            ED = stD.enter_context
            xln = ED(nc.sbuf_tensor("xln" + xin, [128, NF * 1024], BF16))
            wgu = ED(nc.sbuf_tensor("wgu" + xin, [128, 3 * NF * 128], BF16))
            g_s = ED(nc.sbuf_tensor("g_s" + xin, [128, 2 * 512], BF16))
            u_s = ED(nc.sbuf_tensor("u_s" + xin, [128, 512], BF16))
            sgT_s = ED(nc.sbuf_tensor("sgT_s" + xin, [128, NI], F32))
            suT_s = ED(nc.sbuf_tensor("suT_s" + xin, [128, NI], F32))
            sdT_s = ED(nc.sbuf_tensor("sdT_s" + xin, [128, NF], F32))
            wdB = ED(nc.sbuf_tensor("wdB" + xin, [128, 2 * 4 * 512], BF16))
            xrD = ED(nc.sbuf_tensor("xrD" + xin, [128, 2 * 512], F32))
            xtmp = ED(nc.sbuf_tensor("xtmp" + xin, [128, 512], F32))
            stD1 = contextlib.ExitStack()
            xb1 = stD1.enter_context(nc.sbuf_tensor("xb1" + xin, [128, NF * 1024], BF16))
            sqD = stD1.enter_context(nc.sbuf_tensor("sqD" + xin, [128, 4 * 512], BF16))

            @bk.sync
            def _(sy, sn=snap()):
                waitall(sy, sn)
                sW.inc(sy.dma_start(out=sgT_s[:], in_=ap["sgT"]))
                sW.inc(sy.dma_start(out=suT_s[:], in_=ap["suT"]))
                sW.inc(sy.dma_start(out=sdT_s[:], in_=ap["sdT"]))

            @bk.gpsimd
            def _(g, sn=snap()):
                waitall(g, sn)
                src = ap[xin].rearrange("(c p) t -> p c t", p=128)
                i = g.dma_start(out=xb1[:].rearrange("p (c w) -> p c w", c=NF), in_=src)
                sG.inc(i)

            for cg in range(2):
                s0 = 512 * cg

                @bk.tensor
                def _(t, s0=s0, sn=snap()):
                    waitall(t, sn)
                    for fc in range(NF):
                        i = t.matmul(pC[0:1, 0:512], ones_col[:],
                                     xb1[:, fc * 1024 + s0:fc * 1024 + s0 + 512],
                                     start=(fc == 0), stop=(fc == NF - 1))
                    sP.inc(i)

                for blk in range(8):
                    @bk.gpsimd
                    def _(g, s0=s0, blk=blk, sn=snap()):
                        waitall(g, sn)
                        src = (xb1[:].rearrange("p (c t) -> p c t", c=NF)
                               [:, 4 * blk:4 * blk + 4, s0:s0 + 512])
                        dst = sqD[:].rearrange("p (c t) -> p c t", c=4)
                        i = g.tensor_mul(dst, src, src)
                        sO.inc(i)

                    @bk.tensor
                    def _(t, blk=blk, sn=snap()):
                        waitall(t, sn)
                        for f4 in range(4):
                            i = t.matmul(pB[0:1, 0:512], ones_col[:],
                                         sqD[:, f4 * 512:(f4 + 1) * 512],
                                         start=(blk == 0 and f4 == 0),
                                         stop=(blk == 7 and f4 == 3))
                        sP.inc(i)

                sd = stats_math(pC[0:1, 0:512], pB[0:1, 0:512], 512, D)

                @bk.tensor
                def _(t, sd=sd):
                    t.wait_ge(sV.h, sd)
                    t.matmul(pB[:, 512:1024], onesf_row[:], arow[:, 0:512],
                             start=True, stop=True)
                    i = t.matmul(pA[:, 0:512], onesf_neg[:], mrow[:, 0:512],
                                 start=True, stop=True)
                    sP.inc(i)

                @bk.vector
                def _(v, sn=snap()):
                    waitall(v, sn)
                    v.tensor_copy(a_b[:], pB[:, 512:1024])
                    i = v.tensor_copy(nm_b[:], pA[:, 0:512])
                    sV.inc(i)

                for fc in range(NF):
                    @bk.vector
                    def _(v, fc=fc, s0=s0, sn=snap()):
                        waitall(v, sn)
                        xs = xb1[:, fc * 1024 + s0:fc * 1024 + s0 + 512]
                        v.tensor_tensor(out=xtmp[:], in0=xs, in1=nm_b[:], op=ALU.add)
                        i = v.tensor_tensor(out=xln[:, fc * 1024 + s0:fc * 1024 + s0 + 512],
                                            in0=xtmp[:], in1=a_b[:], op=ALU.mult)
                        sV.inc(i)

            barrier()
            stD1.close()
            H = ED(nc.sbuf_tensor("H" + xin, [128, NI * 512], BF16))

            wslot_mark = {}
            silu_mark = {}
            us_mark = {}
            wdslot_mark = {}
            ev_mark = 0
            st_mark = 0
            for hf in range(2):
                t0c = 512 * hf
                for ic in range(NI):
                    sg_ = (2 * ic) % 3
                    su_ = (2 * ic + 1) % 3

                    @bk.gpsimd
                    def _(g, ic=ic, sg_=sg_, su_=su_,
                          m1=wslot_mark.get((2 * ic) % 3, 0),
                          m2=wslot_mark.get((2 * ic + 1) % 3, 0),
                          sn=(snap() if ic == 0 else None)):
                        if sn is not None:
                            waitall(g, sn)
                        if m1:
                            g.wait_ge(sP.h, m1)
                        src = ap["wg_bf"].rearrange("(c p) i -> p c i", p=128)
                        i = g.dma_start(
                            out=wgu[:, sg_ * 4096:(sg_ + 1) * 4096]
                                .rearrange("p (c o) -> p c o", c=NF),
                            in_=src[:, :, 128 * ic:128 * (ic + 1)])
                        sG.inc(i)
                        if m2:
                            g.wait_ge(sP.h, m2)
                        src = ap["wu_bf"].rearrange("(c p) i -> p c i", p=128)
                        i = g.dma_start(
                            out=wgu[:, su_ * 4096:(su_ + 1) * 4096]
                                .rearrange("p (c o) -> p c o", c=NF),
                            in_=src[:, :, 128 * ic:128 * (ic + 1)])
                        sG.inc(i)
                    ld_g = sG.n - 16
                    ld_u = sG.n

                    @bk.tensor
                    def _(t, ic=ic, sg_=sg_, su_=su_, t0c=t0c, ldg=ld_g, ldu=ld_u,
                          sm=silu_mark.get((hf, ic - 2), 0) or silu_mark.get((hf - 1, NI - 2 + ic), 0),
                          um=us_mark.get((hf, ic - 1), 0) or us_mark.get((hf - 1, NI - 1), 0)):
                        t.wait_ge(sG.h, ldg)
                        if sm:
                            t.wait_ge(sA.h, sm)
                        bank = pB[:, 512 * (ic % 2):512 * (ic % 2) + 512]
                        for fc in range(NF):
                            i = t.matmul(bank, wgu[:, sg_ * 4096 + fc * 128:sg_ * 4096 + fc * 128 + 128],
                                         xln[:, fc * 1024 + t0c:fc * 1024 + t0c + 512],
                                         start=(fc == 0), stop=(fc == NF - 1))
                        sP.inc(i)
                        t.wait_ge(sG.h, ldu)
                        if um:
                            t.wait_ge(sV.h, um)
                        for fc in range(NF):
                            i = t.matmul(pC[:], wgu[:, su_ * 4096 + fc * 128:su_ * 4096 + fc * 128 + 128],
                                         xln[:, fc * 1024 + t0c:fc * 1024 + t0c + 512],
                                         start=(fc == 0), stop=(fc == NF - 1))
                        sP.inc(i)
                    g_done, u_done = sP.n - 1, sP.n
                    wslot_mark[sg_] = g_done
                    wslot_mark[su_] = u_done

                    @bk.scalar
                    def _(sc, ic=ic, gd=g_done):
                        sc.wait_ge(sP.h, gd)
                        bank = pB[:, 512 * (ic % 2):512 * (ic % 2) + 512]
                        i = sc.activation(out=g_s[:, 512 * (ic % 2):512 * (ic % 2) + 512],
                                          in_=bank, func=AF.Silu,
                                          scale=sgT_s[:, ic:ic + 1])
                        sA.inc(i)
                    silu_mark[(hf, ic)] = sA.n

                    @bk.vector
                    def _(v, ic=ic, ud=u_done, sm=silu_mark[(hf, ic)]):
                        v.wait_ge(sP.h, ud)
                        v.tensor_scalar(out=u_s[:], in0=pC[:], scalar1=suT_s[:, ic:ic + 1],
                                        scalar2=None, op0=ALU.mult)
                        v.wait_ge(sA.h, sm)
                        i = v.tensor_tensor(out=H[:, 512 * ic:512 * (ic + 1)],
                                            in0=g_s[:, 512 * (ic % 2):512 * (ic % 2) + 512],
                                            in1=u_s[:], op=ALU.mult)
                        sV.inc(i)
                    us_mark[(hf, ic)] = sV.n

                @bk.tensor
                def _(t, sn=snap()):
                    waitall(t, sn)
                nblk = (NI + 3) // 4
                for ob in range(8):
                    for blki in range(nblk):
                        i0 = 4 * blki
                        ni = min(4, NI - i0)
                        slot = blki % 2

                        @bk.gpsimd
                        def _(g, i0=i0, ni=ni, slot=slot, ob=ob,
                              m=wdslot_mark.get(slot, 0)):
                            if m:
                                g.wait_ge(sP.h, m)
                            src = ap["wd_bf"].rearrange("(c p) o -> p c o", p=128)
                            i = g.dma_start(
                                out=wdB[:, slot * 2048:slot * 2048 + ni * 512]
                                    .rearrange("p (c o) -> p c o", c=ni),
                                in_=src[:, i0:i0 + ni, 512 * ob:512 * (ob + 1)])
                            sG.inc(i)
                        ldd = sG.n

                        @bk.tensor
                        def _(t, i0=i0, ni=ni, slot=slot, t0c=t0c, ldd=ldd,
                              em=ev_mark, blki=blki):
                            t.wait_ge(sG.h, ldd)
                            if blki == 0 and em:
                                t.wait_ge(sV.h, em)
                            for k in range(ni):
                                ic = i0 + k
                                for oq in range(4):
                                    i = t.matmul(
                                        pA[:, 512 * oq:512 * oq + 512],
                                        wdB[:, slot * 2048 + k * 512 + 128 * oq:
                                            slot * 2048 + k * 512 + 128 * (oq + 1)],
                                        H[:, 512 * ic:512 * (ic + 1)],
                                        start=(ic == 0), stop=(ic == NI - 1))
                            sP.inc(i)
                        wdslot_mark[slot] = sP.n
                    y_done = sP.n

                    for rnd in range(2):
                        @bk.sync
                        def _(sy, ob=ob, t0c=t0c, rnd=rnd, stm=st_mark):
                            if stm:
                                sy.wait_ge(sS.h, stm)
                            for oq2 in range(2):
                                oc = 4 * ob + 2 * rnd + oq2
                                i = sy.dma_start(out=xrD[:, 512 * oq2:512 * oq2 + 512],
                                                 in_=ap[xin][128 * oc:128 * (oc + 1),
                                                             t0c:t0c + 512])
                                sW.inc(i)
                        xr_ld = sW.n

                        @bk.vector
                        def _(v, ob=ob, rnd=rnd, yd=y_done, xl_=xr_ld):
                            v.wait_ge(sP.h, yd)
                            v.wait_ge(sW.h, xl_)
                            for oq2 in range(2):
                                oq = 2 * rnd + oq2
                                oc = 4 * ob + oq
                                i = v.scalar_tensor_tensor(
                                    out=xrD[:, 512 * oq2:512 * oq2 + 512],
                                    in0=pA[:, 512 * oq:512 * oq + 512],
                                    scalar=sdT_s[:, oc:oc + 1],
                                    in1=xrD[:, 512 * oq2:512 * oq2 + 512],
                                    op0=ALU.mult, op1=ALU.add)
                            sV.inc(i)
                        ev_mark = sV.n

                        @bk.sync
                        def _(sy, ob=ob, rnd=rnd, t0c=t0c, em=ev_mark):
                            sy.wait_ge(sV.h, em)
                            for oq2 in range(2):
                                oc = 4 * ob + 2 * rnd + oq2
                                i = sy.dma_start(out=ap[xout][128 * oc:128 * (oc + 1),
                                                              t0c:t0c + 512],
                                                 in_=xrD[:, 512 * oq2:512 * oq2 + 512])
                                sS.inc(i)
                        st_mark = sS.n
            barrier()
            stD.close()

        for nm, scrname in [("INIT1", "x1_dram"), ("INIT2", "x2_dram"),
                            ("INIT3", "x3_dram")]:
            if nm in phases:
                inp("xinit_" + scrname, [D, OWNT], F32)

                @bk.sync
                def _(sy, scrname=scrname, sn=snap()):
                    waitall(sy, sn)
                    i = sy.dma_start(out=ap[scrname], in_=ap["xinit_" + scrname])
                    sS.inc(i)
                barrier()

        if "D1" in phases:
            ffn_phase("x1_dram", "x2_dram")

        doEm = any(x in phases for x in ("E", "Em", "Ep"))
        doEp = any(x in phases for x in ("E", "Ep"))
        if any(x in phases for x in ("E", "Eq", "Em", "Ep")):
            stE = contextlib.ExitStack()
            EE = stE.enter_context
            qb = EE(nc.sbuf_tensor("qb", [128, 48 * 1024], BF16))
            stE1 = contextlib.ExitStack()
            E1 = stE1.enter_context
            wqbB = E1(nc.sbuf_tensor("wqbB", [128, 2 * NF * 128], BF16))
            a_bE = E1(nc.sbuf_tensor("a_bE", [128, 2 * 512], F32))
            crowE = E1(nc.sbuf_tensor("crowE", [1, 2 * 512], F32))
            w1b = E1(nc.sbuf_tensor("w1b", [1, 128], F32))
            xb2 = E1(nc.sbuf_tensor("xb2", [128, NF * 1024], BF16))
            sqE = E1(nc.sbuf_tensor("sqE", [128, 4 * 512], BF16))

            @bk.gpsimd
            def _(g, sn=snap()):
                waitall(g, sn)
                src = ap["x2_dram"].rearrange("(c p) t -> p c t", p=128)
                i = g.dma_start(out=xb2[:].rearrange("p (c w) -> p c w", c=NF), in_=src)
                sG.inc(i)

            for cg in range(2):
                s0 = 512 * cg

                @bk.tensor
                def _(t, s0=s0, sn=snap()):
                    waitall(t, sn)
                    for fc in range(NF):
                        i = t.matmul(pC[0:1, 0:512], ones_col[:],
                                     xb2[:, fc * 1024 + s0:fc * 1024 + s0 + 512],
                                     start=(fc == 0), stop=(fc == NF - 1))
                    sP.inc(i)

                for blk in range(8):
                    @bk.gpsimd
                    def _(g, s0=s0, blk=blk, sn=snap()):
                        waitall(g, sn)
                        src = (xb2[:].rearrange("p (c t) -> p c t", c=NF)
                               [:, 4 * blk:4 * blk + 4, s0:s0 + 512])
                        dst = sqE[:].rearrange("p (c t) -> p c t", c=4)
                        i = g.tensor_mul(dst, src, src)
                        sO.inc(i)

                    @bk.tensor
                    def _(t, blk=blk, sn=snap()):
                        waitall(t, sn)
                        for f4 in range(4):
                            i = t.matmul(pB[0:1, 0:512], ones_col[:],
                                         sqE[:, f4 * 512:(f4 + 1) * 512],
                                         start=(blk == 0 and f4 == 0),
                                         stop=(blk == 7 and f4 == 3))
                        sP.inc(i)

                sd = stats_math(pC[0:1, 0:512], pB[0:1, 0:512], 512, D)

                @bk.tensor
                def _(t, sd=sd):
                    t.wait_ge(sV.h, sd)
                    i = t.matmul(pB[:, 512:1024], onesf_row[:], arow[:, 0:512],
                                 start=True, stop=True)
                    sP.inc(i)

                @bk.vector
                def _(v, cg=cg, sn=snap()):
                    waitall(v, sn)
                    v.tensor_copy(a_bE[:, 512 * cg:512 * cg + 512], pB[:, 512:1024])
                    i = v.tensor_copy(crowE[:, 512 * cg:512 * cg + 512], crow[:, 0:512])
                    sV.inc(i)

            qkv_pe_mark = {}
            qkv_epi_mark = 0
            for oc in range(48):
                slot = oc % 2

                @bk.gpsimd
                def _(g, oc=oc, slot=slot, m=qkv_pe_mark.get(oc - 2, 0),
                      sn=(snap() if oc == 0 else None)):
                    if sn is not None:
                        waitall(g, sn)
                    if m:
                        g.wait_ge(sP.h, m)
                    src = ap["wb_qkvT"].rearrange("(c p) o -> p c o", p=128)
                    i = g.dma_start(out=wqbB[:, slot * 4096:(slot + 1) * 4096]
                                    .rearrange("p (c o) -> p c o", c=NF),
                                    in_=src[:, :, 128 * oc:128 * (oc + 1)])
                    sG.inc(i)
                ldq = sG.n

                @bk.tensor
                def _(t, oc=oc, slot=slot, ldq=ldq, em=qkv_epi_mark):
                    t.wait_ge(sG.h, ldq)
                    wsl = wqbB[:, slot * 4096:(slot + 1) * 4096]
                    for fc in range(NF):
                        i = t.matmul(pC[0:1, 0:128], ones_col[:],
                                     wsl[fc * 128:(fc + 1) * 128] if False else
                                     wsl[:, fc * 128:(fc + 1) * 128],
                                     start=(fc == 0), stop=(fc == NF - 1))
                    sP.inc(i)
                w1_p = sP.n

                @bk.vector
                def _(v, p=w1_p):
                    v.wait_ge(sP.h, p)
                    i = v.tensor_copy(w1b[:], pC[0:1, 0:128])
                    sV.inc(i)
                w1_v = sV.n

                @bk.tensor
                def _(t, oc=oc, slot=slot, em=qkv_epi_mark, wv=w1_v):
                    if em:
                        t.wait_ge(sV.h, em)
                    wsl = wqbB[:, slot * 4096:(slot + 1) * 4096]
                    for cg in range(2):
                        bank = pA[:, 1024 * cg:1024 * cg + 512]
                        for fc in range(NF):
                            t.matmul(bank, wsl[:, fc * 128:(fc + 1) * 128],
                                     xb2[:, fc * 1024 + 512 * cg:fc * 1024 + 512 * cg + 512],
                                     start=(fc == 0), stop=False)
                    t.wait_ge(sV.h, wv)
                    for cg in range(2):
                        bank = pA[:, 1024 * cg:1024 * cg + 512]
                        i = t.matmul(bank, w1b[:], crowE[:, 512 * cg:512 * cg + 512],
                                     start=False, stop=True)
                    sP.inc(i)
                qkv_pe_mark[oc] = sP.n
                mains_p = sP.n

                @bk.vector
                def _(v, oc=oc, p=mains_p):
                    v.wait_ge(sP.h, p)
                    for cg in range(2):
                        bank = pA[:, 1024 * cg:1024 * cg + 512]
                        i = v.tensor_tensor(
                            out=qb[:, oc * 1024 + 512 * cg:oc * 1024 + 512 * cg + 512],
                            in0=bank, in1=a_bE[:, 512 * cg:512 * cg + 512], op=ALU.mult)
                    sV.inc(i)
                qkv_epi_mark = sV.n

            barrier()
            if "erows" in dbg:
                @bk.sync
                def _(sy):
                    i = sy.dma_start(out=ap["dbg_erows"][0:1, 0:128], in_=w1b[:])
                    sS.inc(i)
                    i = sy.dma_start(out=ap["dbg_erows"][1:2, 0:1024], in_=crowE[:])
                    sS.inc(i)
                barrier()
            if "qb" in dbg:
                @bk.sync
                def _(sy):
                    i = sy.dma_start(out=ap["dbg_qb"], in_=qb[:])
                    sS.inc(i)
                barrier()
            stE1.close()
            AOB = EE(nc.sbuf_tensor("AOB", [128, NF * 1024], BF16))
            stE2 = contextlib.ExitStack()
            E2 = stE2.enter_context
            hsel_s = E2(nc.sbuf_tensor("hsel_s", [128, 1024], BF16))
            seh_s = E2(nc.sbuf_tensor("seh_s", [32, 4096], BF16))
            prod2 = E2(nc.sbuf_tensor("prod2", [128, 2 * 256], BF16))
            esE = E2(nc.sbuf_tensor("esE", [32, 4 * 256], F32))
            mxE = E2(nc.sbuf_tensor("mxE", [32, 256], F32))
            rlE = E2(nc.sbuf_tensor("rlE", [32, 256], F32))
            pbsE = E2(nc.sbuf_tensor("pbsE", [32, 4 * 256], BF16))
            pbb = E2(nc.sbuf_tensor("pbb", [128, 256], BF16))
            vtmp = E2(nc.sbuf_tensor("vtmp", [128, 256], F32))
            accE = E2(nc.sbuf_tensor("accE", [128, 256], F32))

            @bk.sync
            def _(sy, sn=snap()):
                waitall(sy, sn)
                sW.inc(sy.dma_start(out=hsel_s[:], in_=ap["hsel"]))
                sW.inc(sy.dma_start(out=seh_s[:], in_=ap["seh"]))

            for i_s in range(4 if doEm else 0):
                for j_s in range(i_s + 1):
                    pm = {}
                    for hc in range(32):
                        slot = hc % 2

                        @bk.vector
                        def _(v, i_s=i_s, j_s=j_s, hc=hc, slot=slot,
                              m=pm.get(hc - 2, 0), sn=(snap() if hc == 0 else None)):
                            if sn is not None:
                                waitall(v, sn)
                            if m:
                                v.wait_ge(sP.h, m)
                            g_ = hc // 4
                            qsl = qb[:, hc * 1024 + i_s * 256:hc * 1024 + i_s * 256 + 256]
                            ksl_ = qb[:, (32 + g_) * 1024 + j_s * 256:
                                       (32 + g_) * 1024 + j_s * 256 + 256]
                            i = v.tensor_tensor(out=prod2[:, slot * 256:slot * 256 + 256],
                                                in0=qsl, in1=ksl_, op=ALU.mult)
                            sV.inc(i)
                        pv = sV.n

                        @bk.tensor
                        def _(t, j_s=j_s, hc=hc, slot=slot, pv=pv):
                            t.wait_ge(sV.h, pv)
                            i = t.matmul(pA[0:32, 512 * j_s:512 * j_s + 256],
                                         hsel_s[:, 32 * hc:32 * hc + 32],
                                         prod2[:, slot * 256:slot * 256 + 256],
                                         start=(hc == 0), stop=(hc == 31))
                            sP.inc(i)
                        pm[hc] = sP.n

                @bk.vector
                def _(v, i_s=i_s, sn=snap()):
                    waitall(v, sn)
                    v.tensor_copy(mxE[:], pA[0:32, 0:256])
                    for j in range(1, i_s + 1):
                        v.tensor_tensor(out=mxE[:], in0=pA[0:32, 512 * j:512 * j + 256],
                                        in1=mxE[:], op=ALU.max)
                    for j in range(i_s + 1):
                        i = v.tensor_tensor(out=esE[:, 256 * j:256 * (j + 1)],
                                            in0=pA[0:32, 512 * j:512 * j + 256],
                                            in1=mxE[:], op=ALU.subtract)
                    sV.inc(i)
                es_v = sV.n

                @bk.scalar
                def _(sc, i_s=i_s, ev=es_v):
                    sc.wait_ge(sV.h, ev)
                    i = sc.activation(out=esE[:, 0:256 * (i_s + 1)],
                                      in_=esE[:, 0:256 * (i_s + 1)], func=AF.Exp,
                                      scale=1.0 / float(np.sqrt(128.0)))
                    sA.inc(i)
                exp_a = sA.n

                @bk.vector
                def _(v, i_s=i_s, ea=exp_a):
                    v.wait_ge(sA.h, ea)
                    v.tensor_copy(rlE[:], esE[:, 0:256])
                    for j in range(1, i_s + 1):
                        v.tensor_tensor(out=rlE[:], in0=esE[:, 256 * j:256 * (j + 1)],
                                        in1=rlE[:], op=ALU.add)
                    v.reciprocal(rlE[:], rlE[:])
                    for j in range(i_s + 1):
                        i = v.tensor_tensor(out=pbsE[:, 256 * j:256 * (j + 1)],
                                            in0=esE[:, 256 * j:256 * (j + 1)],
                                            in1=rlE[:], op=ALU.mult)
                    sV.inc(i)

                for hc in range(32):
                    @bk.tensor
                    def _(t, i_s=i_s, hc=hc, sn=snap()):
                        waitall(t, sn)
                        for j in range(i_s + 1):
                            i = t.matmul(pB[:, 256 * j:256 * j + 256],
                                         seh_s[:, 128 * hc:128 * (hc + 1)],
                                         pbsE[0:32, 256 * j:256 * (j + 1)],
                                         start=True, stop=True)
                        sP.inc(i)
                    bc_p = sP.n

                    for j in range(i_s + 1):
                        @bk.vector
                        def _(v, j=j, bp=bc_p, om=sO.n):
                            v.wait_ge(sP.h, bp)
                            if om:
                                v.wait_ge(sO.h, om)
                            i = v.tensor_copy(pbb[:], pB[:, 256 * j:256 * j + 256])
                            sV.inc(i)
                        bb_v = sV.n

                        @bk.gpsimd
                        def _(g, i_s=i_s, j=j, hc=hc, bv=bb_v):
                            g.wait_ge(sV.h, bv)
                            g_ = hc // 4
                            vsl_ = qb[:, (40 + g_) * 1024 + j * 256:
                                      (40 + g_) * 1024 + j * 256 + 256]
                            g.tensor_mul(vtmp[:], vsl_, pbb[:])
                            if j == 0:
                                i = g.tensor_copy(accE[:], vtmp[:])
                            else:
                                i = g.tensor_add(accE[:], accE[:], vtmp[:])
                            if j == i_s:
                                i = g.tensor_copy(
                                    AOB[:, hc * 1024 + i_s * 256:hc * 1024 + i_s * 256 + 256],
                                    accE[:])
                            sO.inc(i)

            barrier()
            if "AOB" in dbg:
                @bk.sync
                def _(sy):
                    i = sy.dma_start(out=ap["dbg_AOB"], in_=AOB[:])
                    sS.inc(i)
                barrier()
            stE2.close()
            xrE = EE(nc.sbuf_tensor("xrE", [128, 2 * 512], F32))
            wqbP = EE(nc.sbuf_tensor("wqbP", [128, NF * 128], BF16))

            for oc in range(NF if doEp else 0):
                @bk.gpsimd
                def _(g, oc=oc, sn=snap()):
                    waitall(g, sn)
                    src = ap["wb_projT"].rearrange("(c p) o -> p c o", p=128)
                    i = g.dma_start(out=wqbP[:].rearrange("p (c o) -> p c o", c=NF),
                                    in_=src[:, :, 128 * oc:128 * (oc + 1)])
                    sG.inc(i)

                @bk.sync
                def _(sy, oc=oc, sn=snap()):
                    waitall(sy, sn)
                    sW.inc(sy.dma_start(out=xrE[:, 0:512],
                                        in_=ap["x2_dram"][128 * oc:128 * (oc + 1), 0:512]))
                    sW.inc(sy.dma_start(out=xrE[:, 512:1024],
                                        in_=ap["x2_dram"][128 * oc:128 * (oc + 1), 512:1024]))

                for cg in range(2):
                    @bk.tensor
                    def _(t, oc=oc, cg=cg, sn=snap()):
                        waitall(t, sn)
                        bank = pA[:, 1024 * cg:1024 * cg + 512]
                        for fc in range(NF):
                            i = t.matmul(bank, wqbP[:, fc * 128:(fc + 1) * 128],
                                         AOB[:, fc * 1024 + 512 * cg:fc * 1024 + 512 * cg + 512],
                                         start=(fc == 0), stop=(fc == NF - 1))
                        sP.inc(i)

                    @bk.vector
                    def _(v, oc=oc, cg=cg, sn=snap()):
                        waitall(v, sn)
                        bank = pA[:, 1024 * cg:1024 * cg + 512]
                        i = v.tensor_tensor(out=xrE[:, 512 * cg:512 * cg + 512],
                                            in0=bank, in1=xrE[:, 512 * cg:512 * cg + 512],
                                            op=ALU.add)
                        sV.inc(i)

                    @bk.sync
                    def _(sy, oc=oc, cg=cg, sn=snap()):
                        waitall(sy, sn)
                        i = sy.dma_start(out=ap["x3_dram"][128 * oc:128 * (oc + 1),
                                                           512 * cg:512 * cg + 512],
                                         in_=xrE[:, 512 * cg:512 * cg + 512])
                        sS.inc(i)

            barrier()
            stE.close()

        if "D2" in phases:
            ffn_phase("x3_dram", "y")

        barrier()
        dram_dbg = [n for n in dbg if n in ("K_dram", "V_dram", "x1_dram",
                                            "x2_dram", "x3_dram")]
        if dram_dbg:
            @bk.sync
            def _(sy):
                for name in dram_dbg:
                    i = sy.dma_start(out=ap["dbg_" + name], in_=ap[name])
                    sS.inc(i)

        @bk.sync
        def _(sy):
            sy.wait_ge(sS.h, sS.n)
    return nc, ap


def kernel(**inputs):
    shared, percore = host_prep(inputs)
    nc, _ = build_nc(dbg=tuple(_DBG))
    in_maps = []
    for j in range(NCORE):
        m = dict(shared)
        m.update(percore[j])
        in_maps.append(m)
    res = run_bass_kernel_spmd(nc, in_maps, core_ids=list(range(NCORE)))
    if _DBG:
        kernel.dbg_out = [{k: v for k, v in r.items() if k.startswith("dbg_")}
                          for r in res.results]
    parts = np.stack([res.results[j]["y"] for j in range(NCORE)])
    y = parts.reshape(NCORE, D, B, OWNP).transpose(2, 0, 3, 1).reshape(B, S, D)
    return np.ascontiguousarray(y)


# version 06 (final)
# speedup vs baseline: 1.0800x; 1.0800x over previous
import contextlib
import numpy as np
import ml_dtypes
import concourse.bass as bass
import concourse.mybir as mybir
from concourse.bass_utils import run_bass_kernel_spmd

F32 = mybir.dt.float32
BF16 = mybir.dt.bfloat16
I32 = mybir.dt.int32
AF = mybir.ActivationFunctionType
ALU = mybir.AluOpType
AX = mybir.AxisListType
BF = ml_dtypes.bfloat16

B, S, D, T = 4, 2048, 4096, 8192
NCORE, OWNP, OWNT = 8, 256, 1024
HQ, HKV, DH, LAT, KVD = 16, 4, 64, 1024, 256
IFF, NI, NF = 11008, 86, 32
EPS = 1e-5
NT, TW = 16, 512

_DBG = []
_PHASES = ("A", "B", "C", "D1", "E", "D2")



def _dei64():
    p = np.zeros(64, np.int64)
    p[:32] = np.arange(32) * 2
    p[32:] = np.arange(32) * 2 + 1
    return p


def _qperm():
    d = _dei64()
    return np.concatenate([h * 64 + d for h in range(HQ)])


def _kperm():
    d = _dei64()
    return np.concatenate([g * 64 + d for g in range(HKV)])


def _dense_conv_T(w, groups, perm):
    O, Ig, K = w.shape
    out = np.zeros((K, O, O), np.float32)
    for tap in range(K):
        for c in range(O):
            g = c // (O // groups)
            out[tap, g * Ig:(g + 1) * Ig, c] = w[c, :, tap]
    return np.ascontiguousarray(out[:, perm][:, :, perm])


def host_prep(inp):
    x = np.asarray(inp["L0_attn_a_x"], np.float32)
    qp, kp = _qperm(), _kperm()

    x_t = np.ascontiguousarray(x.transpose(2, 0, 1).reshape(D, T))

    w_qkv = np.asarray(inp["L0_attn_a_w_qkv"], np.float32)
    wa_qT = np.ascontiguousarray(w_qkv.T[:, qp])
    kv_cols = np.concatenate([1024 + kp, np.arange(1280, 1536)])
    wa_kvT = np.ascontiguousarray(w_qkv.T[:, kv_cols])
    wa_oT = np.ascontiguousarray(np.asarray(inp["L0_attn_a_w_o"], np.float32).T)

    qc1dT = _dense_conv_T(np.asarray(inp["L0_attn_a_qc1_w"], np.float32), HQ, qp)
    qc2dT = _dense_conv_T(np.asarray(inp["L0_attn_a_qc2_w"], np.float32), 1, qp)
    kc1dT = _dense_conv_T(np.asarray(inp["L0_attn_a_kc1_w"], np.float32), HKV, kp)
    kc2dT = _dense_conv_T(np.asarray(inp["L0_attn_a_kc2_w"], np.float32), 1, kp)
    qc1b = np.asarray(inp["L0_attn_a_qc1_b"], np.float32)[qp]
    qc2b = np.asarray(inp["L0_attn_a_qc2_b"], np.float32)[qp]
    kc1b = np.asarray(inp["L0_attn_a_kc1_b"], np.float32)[kp]
    kc2b = np.asarray(inp["L0_attn_a_kc2_b"], np.float32)[kp]

    inv = 1.0 / 10000.0 ** (np.arange(0, DH, 2, dtype=np.float32) / DH)
    ang = np.arange(S, dtype=np.float32)[:, None] * inv[None, :]
    scos, ssin = np.cos(ang).T.astype(np.float32), np.sin(ang).T.astype(np.float32)
    r32 = np.arange(128) % 32
    sgn = np.where(((np.arange(128) // 32) % 2) == 0, -1.0, 1.0).astype(np.float32)
    cos2 = np.ascontiguousarray(np.tile(scos[r32], (1, B)))
    sin2 = np.ascontiguousarray(np.tile(ssin[r32], (1, B)) * sgn[:, None])

    wb_qkv = np.asarray(inp["L0_attn_b_weight_qkv"], np.float32)
    cols = np.zeros(6144, np.int64)
    for m in range(4096):
        h, dd = m // 128, m % 128
        cols[m] = (h // 4) * 768 + (h % 4) * 128 + dd
    for m in range(1024):
        g, dd = m // 128, m % 128
        cols[4096 + m] = g * 768 + 512 + dd
        cols[5120 + m] = g * 768 + 640 + dd
    wb_qkvT = np.ascontiguousarray(wb_qkv.T[:, cols])
    wb_projT = np.ascontiguousarray(np.asarray(inp["L0_attn_b_weight_proj"], np.float32).T)

    wgT = np.ascontiguousarray(np.asarray(inp["L0_ffn_w_gate"], np.int32).T)
    wuT = np.ascontiguousarray(np.asarray(inp["L0_ffn_w_up"], np.int32).T)
    wdT = np.ascontiguousarray(np.asarray(inp["L0_ffn_w_down"], np.int32).T)
    sgTc = np.ascontiguousarray(np.asarray(inp["L0_ffn_s_gate"], np.float32).reshape(NI, 128).T)
    suTc = np.ascontiguousarray(np.asarray(inp["L0_ffn_s_up"], np.float32).reshape(NI, 128).T)
    sdTc = np.ascontiguousarray(np.asarray(inp["L0_ffn_s_down"], np.float32).reshape(NF, 128).T)

    eye = np.eye(128, dtype=BF)
    sel64 = np.zeros((64, 128), dtype=BF)
    sel64[np.arange(128) % 64, np.arange(128)] = 1
    selkf = np.zeros((2, 128), dtype=np.float32)
    selkf[np.arange(128) // 64, np.arange(128)] = 1
    bdo = np.zeros((128, 2), dtype=BF)
    bdo[np.arange(128), np.arange(128) // 64] = 1
    hsel = np.zeros((128, 32 * 32), dtype=BF)
    for h in range(32):
        hsel[:, h * 32 + h] = 1
    seh = np.zeros((32, 32 * 128), dtype=BF)
    for h in range(32):
        seh[h, h * 128:(h + 1) * 128] = 1

    shared = dict(
        x_t=x_t, wa_qT=wa_qT, wa_kvT=wa_kvT, wa_oT=wa_oT,
        qc1dT=qc1dT, qc2dT=qc2dT, kc1dT=kc1dT, kc2dT=kc2dT,
        qc1b=np.ascontiguousarray(qc1b.reshape(8, 128).T),
        qc2b=np.ascontiguousarray(qc2b.reshape(8, 128).T),
        kc1b=np.ascontiguousarray(kc1b.reshape(2, 128).T),
        kc2b=np.ascontiguousarray(kc2b.reshape(2, 128).T),
        key_temp=np.asarray(inp["L0_attn_a_key_temp"], np.float32).reshape(1, 1),
        cos2=cos2, sin2=sin2,
        wb_qkvT=wb_qkvT, wb_projT=wb_projT,
        wgT=wgT, wuT=wuT, wdT=wdT, sgT=sgTc, suT=suTc, sdT=sdTc,
        eye=eye, sel64=sel64, selkf=selkf, bdo=bdo, hsel=hsel, seh=seh,
    )

    percore = []
    for j in range(NCORE):
        own_pos = np.arange(OWNP * j, OWNP * (j + 1))
        own_cols = (np.arange(B)[:, None] * S + own_pos[None, :]).reshape(-1)
        x_own = np.ascontiguousarray(x_t[:, own_cols])
        xh = np.zeros((D, B * 260), np.float32)
        for b in range(B):
            lo = OWNP * j - 4
            src0 = max(lo, 0)
            off = src0 - lo
            xh[:, b * 260 + off:(b + 1) * 260] = x_t[:, b * S + src0:b * S + OWNP * (j + 1)]
        mask = np.zeros((128, 16 * 256), dtype=BF)
        tq = OWNP * j + np.arange(256)
        for i in range(16):
            tk = 128 * i + np.arange(128)
            mask[:, i * 256:(i + 1) * 256] = (tk[:, None] <= tq[None, :]).astype(BF)
        percore.append(dict(
            x_own=x_own, x_ownh=xh, mask=np.ascontiguousarray(mask),
            cos2o=np.ascontiguousarray(cos2[:, own_cols]),
            sin2o=np.ascontiguousarray(sin2[:, own_cols]),
        ))
    return shared, percore



class Ch:
    def __init__(self, handle, step=1):
        self.h, self.n, self.step = handle, 0, step

    def inc(self, instr, k=None):
        k = self.step if k is None else k
        instr.then_inc(self.h, k)
        self.n += k
        return self.n


def build_nc(dbg=(), phases=_PHASES):
    nc = bass.Bass("TRN2", target_bir_lowering=False)
    ap = {}

    def inp(name, shape, dt):
        ap[name] = nc.dram_tensor(name, shape, dt, kind="ExternalInput").ap()

    def outp(name, shape, dt):
        ap[name] = nc.dram_tensor(name, shape, dt, kind="ExternalOutput").ap()

    def scr(name, shape, dt):
        ap[name] = nc.dram_tensor(name, shape, dt, kind="Internal").ap()

    inp("x_t", [D, T], F32)
    inp("x_own", [D, OWNT], F32)
    inp("x_ownh", [D, B * 260], F32)
    inp("wa_qT", [D, LAT], F32)
    inp("wa_kvT", [D, 512], F32)
    inp("wa_oT", [LAT, D], F32)
    inp("qc1dT", [3, LAT, LAT], F32)
    inp("qc2dT", [3, LAT, LAT], F32)
    inp("kc1dT", [3, KVD, KVD], F32)
    inp("kc2dT", [3, KVD, KVD], F32)
    inp("qc1b", [128, 8], F32)
    inp("qc2b", [128, 8], F32)
    inp("kc1b", [128, 2], F32)
    inp("kc2b", [128, 2], F32)
    inp("key_temp", [1, 1], F32)
    inp("cos2", [128, T], F32)
    inp("sin2", [128, T], F32)
    inp("cos2o", [128, OWNT], F32)
    inp("sin2o", [128, OWNT], F32)
    inp("mask", [128, 16 * 256], BF16)
    inp("wb_qkvT", [D, 6144], F32)
    inp("wb_projT", [D, D], F32)
    inp("wgT", [D, IFF], I32)
    inp("wuT", [D, IFF], I32)
    inp("wdT", [IFF, D], I32)
    inp("sgT", [128, NI], F32)
    inp("suT", [128, NI], F32)
    inp("sdT", [128, NF], F32)
    inp("eye", [128, 128], BF16)
    inp("sel64", [64, 128], BF16)
    inp("selkf", [2, 128], F32)
    inp("bdo", [128, 2], BF16)
    inp("hsel", [128, 32 * 32], BF16)
    inp("seh", [32, 32 * 128], BF16)

    outp("y", [D, OWNT], F32)

    scr("K_dram", [KVD, T], BF16)
    scr("wg_bf", [D, IFF], BF16)
    scr("wu_bf", [D, IFF], BF16)
    scr("wd_bf", [IFF, D], BF16)
    scr("V_dram", [T, KVD], BF16)
    scr("x1_dram", [D, OWNT], F32)
    scr("x2_dram", [D, OWNT], F32)
    scr("x3_dram", [D, OWNT], F32)
    dbg_shapes = {"K_dram": ([KVD, T], BF16), "V_dram": ([T, KVD], BF16),
                  "x1_dram": ([D, OWNT], F32), "x2_dram": ([D, OWNT], F32),
                  "x3_dram": ([D, OWNT], F32), "qb": ([128, 48 * 1024], BF16),
                  "AOB": ([128, NF * 1024], BF16), "erows": ([2, 1024], F32)}
    for name in dbg:
        shp, dt = dbg_shapes[name]
        outp("dbg_" + name, shp, dt)

    with contextlib.ExitStack() as st:
        E = st.enter_context

        eye_s = E(nc.sbuf_tensor("eye_s", [128, 128], BF16))
        sel64_s = E(nc.sbuf_tensor("sel64_s", [64, 128], BF16))
        selkf_s = E(nc.sbuf_tensor("selkf_s", [2, 128], F32))
        bdo_s = E(nc.sbuf_tensor("bdo_s", [128, 2], BF16))
        ones_col = E(nc.sbuf_tensor("ones_col", [128, 1], BF16))
        onesf_row = E(nc.sbuf_tensor("onesf_row", [1, 128], F32))
        ktb = E(nc.sbuf_tensor("ktb", [1, 1], F32))
        ktraw = E(nc.sbuf_tensor("ktraw", [1, 1], F32))
        negB = E(nc.sbuf_tensor("negB", [128, 1], F32))
        kt2 = E(nc.sbuf_tensor("kt2", [2, 1], F32))
        epsb = E(nc.sbuf_tensor("epsb", [1, 1], F32))
        onesf_neg = E(nc.sbuf_tensor("onesf_neg", [1, 128], F32))
        statrows = E(nc.sbuf_tensor("statrows", [1, 4 * TW], F32))
        mrow = statrows[:, 0 * TW:1 * TW]
        tsq = statrows[:, 1 * TW:2 * TW]
        varrow = tsq
        arow = statrows[:, 2 * TW:3 * TW]
        crow = statrows[:, 3 * TW:4 * TW]
        a_b = E(nc.sbuf_tensor("a_b", [128, TW], F32))
        nm_b = E(nc.sbuf_tensor("nm_b", [128, TW], F32))
        stBC = contextlib.ExitStack()
        Q = stBC.enter_context(nc.sbuf_tensor("Q", [128, 8 * 1024], BF16))
        stQM = contextlib.ExitStack()
        qmT = stQM.enter_context(nc.sbuf_tensor("qmT", [128, NF * 64], BF16))
        kmT = stQM.enter_context(nc.sbuf_tensor("kmT", [128, NF * 64], BF16))
        w1pack = stQM.enter_context(nc.sbuf_tensor("w1pack", [1, 512 + 64 + LAT + 64], F32))
        w1kv = w1pack[:, 0:512]
        w1qm = w1pack[:, 512:576]
        w1q = w1pack[:, 576:576 + LAT]
        w1km = w1pack[:, 576 + LAT:576 + LAT + 64]

        pA = E(nc.psum_tensor("pA", [128, 2048], F32))
        pB = E(nc.psum_tensor("pB", [128, 1024], F32))
        pC = E(nc.psum_tensor("pC", [128, 512], F32))
        pD = E(nc.psum_tensor("pD", [128, 1024], BF16))

        sW = Ch(E(nc.semaphore("sW")), 16)
        sG = Ch(E(nc.semaphore("sG")), 16)
        sP = Ch(E(nc.semaphore("sP")), 1)
        sV = Ch(E(nc.semaphore("sV")), 1)
        sA = Ch(E(nc.semaphore("sA")), 1)
        sO = Ch(E(nc.semaphore("sO")), 1)
        sS = Ch(E(nc.semaphore("sS")), 16)
        CHANS = [sW, sG, sP, sV, sA, sO, sS]

        bk = E(nc.Block())

        def barrier():
            snap = [(c.h, c.n) for c in CHANS]
            for eng in (bk.tensor, bk.vector, bk.scalar, bk.gpsimd, bk.sync):
                @eng
                def _(e, snap=snap):
                    for h, n in snap:
                        if n:
                            e.wait_ge(h, n)

        def stats_math(ps_S, ps_Q, w, nfeat):
            qd = sP.n

            @bk.scalar
            def _(sc, qd=qd, w=w):
                sc.wait_ge(sP.h, qd)
                sc.mul(mrow[:, 0:w], ps_S, 1.0 / nfeat)
                i = sc.activation(out=tsq[:, 0:w], in_=mrow[:, 0:w], func=AF.Square)
                sA.inc(i)
            aa = sA.n

            @bk.vector
            def _(v, aa=aa, w=w):
                v.wait_ge(sA.h, aa)
                i = v.scalar_tensor_tensor(out=varrow[:, 0:w], in0=ps_Q,
                                           scalar=1.0 / nfeat, in1=tsq[:, 0:w],
                                           op0=ALU.mult, op1=ALU.subtract)
                sV.inc(i)
            vv = sV.n

            @bk.scalar
            def _(sc, vv=vv, w=w):
                sc.wait_ge(sV.h, vv)
                i = sc.activation(out=arow[:, 0:w], in_=varrow[:, 0:w], func=AF.Sqrt,
                                  bias=epsb[:])
                sA.inc(i)
            aa2 = sA.n

            @bk.vector
            def _(v, aa2=aa2, w=w):
                v.wait_ge(sA.h, aa2)
                v.reciprocal(arow[:, 0:w], arow[:, 0:w])
                i = v.tensor_scalar(out=crow[:, 0:w], in0=mrow[:, 0:w], scalar1=-1.0,
                                    scalar2=None, op0=ALU.mult)
                sV.inc(i)
            return sV.n

        @bk.sync
        def _(sy):
            for dst, src in [(eye_s, "eye"), (sel64_s, "sel64"), (selkf_s, "selkf"),
                             (bdo_s, "bdo"), (ktb, "key_temp"), (ktraw, "key_temp")]:
                i = sy.dma_start(out=dst[:], in_=ap[src])
                sW.inc(i)
        prep_w = sW.n

        @bk.gpsimd
        def _(g):
            g.memset(ones_col[:], 1.0)
            g.memset(epsb[:], EPS)
            g.memset(onesf_neg[:], -1.0)
            i = g.memset(onesf_row[:], 1.0)
            sO.inc(i)
        ones_done = sO.n


        @bk.scalar
        def _(sc):
            sc.wait_ge(sW.h, prep_w)
            i = sc.activation(out=ktb[:], in_=ktb[:], func=AF.Abs, scale=0.13125)
            sA.inc(i)
        ktb_done = sA.n

        @bk.tensor
        def _(t):
            t.wait_ge(sA.h, ktb_done)
            t.wait_ge(sO.h, ones_done)
            t.matmul(pC[:, 0:1], onesf_row[:], ktb[:], start=True, stop=True)
            i = t.matmul(pC[0:2, 4:5], onesf_row[:, 0:2], ktraw[:], start=True, stop=True)
            sP.inc(i)
        ktp = sP.n

        @bk.vector
        def _(v):
            v.wait_ge(sP.h, ktp)
            v.tensor_scalar(out=negB[:], in0=pC[0:128, 0:1], scalar1=-1.0,
                            scalar2=None, op0=ALU.mult)
            i = v.tensor_copy(kt2[:], pC[0:2, 4:5])
            sV.inc(i)
        ktv = sV.n

        stA = contextlib.ExitStack()
        wkv_sb = stA.enter_context(nc.sbuf_tensor("wkv_sb", [128, NF * 512], BF16))
        stWQ = contextlib.ExitStack()
        wq_t2 = stWQ.enter_context(nc.sbuf_tensor("wq_t2", [128, 2 * LAT], BF16))
        qmtmp = stWQ.enter_context(nc.sbuf_tensor("qmtmp", [128, 64], F32))

        fc_pe_marks = {}
        for fc in range(NF):
            slot = fc % 2

            @bk.gpsimd
            def _(g, fc=fc, slot=slot, vn=sV.n,
                  pm=(fc_pe_marks.get(fc - 2, 0))):
                if fc >= 2:
                    g.wait_ge(sV.h, vn)
                    g.wait_ge(sP.h, pm)
                i = g.dma_start(out=wq_t2[:, slot * LAT:(slot + 1) * LAT],
                                in_=ap["wa_qT"][128 * fc:128 * (fc + 1), :])
                sG.inc(i)
                i = g.dma_start(out=wkv_sb[:, 512 * fc:512 * (fc + 1)],
                                in_=ap["wa_kvT"][128 * fc:128 * (fc + 1), :])
                sG.inc(i)
            gn = sG.n

            @bk.vector
            def _(v, fc=fc, slot=slot, gn=gn):
                v.wait_ge(sG.h, gn)
                wq = wq_t2[:, slot * LAT:(slot + 1) * LAT]
                src = wq.rearrange("p (h d) -> p d h", h=16)
                v.tensor_reduce(out=qmtmp[:], in_=src, axis=AX.X, op=ALU.add)
                v.tensor_scalar(out=qmT[:, 64 * fc:64 * (fc + 1)], in0=qmtmp[:],
                                scalar1=1.0 / 16, scalar2=None, op0=ALU.mult)
                srck = wkv_sb[:, 512 * fc:512 * fc + 256].rearrange("p (g d) -> p d g", g=4)
                v.tensor_reduce(out=qmtmp[:], in_=srck, axis=AX.X, op=ALU.add)
                i = v.tensor_scalar(out=kmT[:, 64 * fc:64 * (fc + 1)], in0=qmtmp[:],
                                    scalar1=0.25, scalar2=None, op0=ALU.mult)
                sV.inc(i)
            vn = sV.n

            @bk.tensor
            def _(t, fc=fc, slot=slot, gn=gn, vn=vn):
                t.wait_ge(sG.h, gn)
                t.wait_ge(sV.h, vn)
                st_, sp_ = (fc == 0), (fc == NF - 1)
                wq = wq_t2[:, slot * LAT:(slot + 1) * LAT]
                t.matmul(pB[0:1, 0:512], ones_col[:], wkv_sb[:, 512 * fc:512 * (fc + 1)],
                         start=st_, stop=sp_)
                t.matmul(pB[0:1, 512:576], ones_col[:], qmT[:, 64 * fc:64 * (fc + 1)],
                         start=st_, stop=sp_)
                t.matmul(pC[0:1, 0:512], ones_col[:], wq[:, 0:512], start=st_, stop=sp_)
                t.matmul(pA[0:1, 0:512], ones_col[:], wq[:, 512:1024], start=st_, stop=sp_)
                i = t.matmul(pA[0:1, 512:576], ones_col[:], kmT[:, 64 * fc:64 * (fc + 1)],
                             start=st_, stop=sp_)
                sP.inc(i)
            fc_pe_marks[fc] = sP.n
        w1_p = sP.n

        @bk.vector
        def _(v):
            v.wait_ge(sP.h, w1_p)
            v.tensor_copy(w1kv[:], pB[0:1, 0:512])
            v.tensor_copy(w1qm[:], pB[0:1, 512:576])
            v.tensor_copy(w1q[:, 0:512], pC[0:1, 0:512])
            v.tensor_copy(w1q[:, 512:1024], pA[0:1, 0:512])
            i = v.tensor_copy(w1km[:], pA[0:1, 512:576])
            sV.inc(i)
        barrier()
        stWQ.close()

        if "A" in phases:
            EA = stA.enter_context
            kcw = EA(nc.sbuf_tensor("kcw", [128, 24 * 128], BF16))
            kc1b_s = EA(nc.sbuf_tensor("kc1b_s", [128, 2], F32))
            kc2b_s = EA(nc.sbuf_tensor("kc2b_s", [128, 2], F32))
            xb_t = EA(nc.sbuf_tensor("xb_t", [128, 2 * NF * TW], BF16))
            sq_t = EA(nc.sbuf_tensor("sq_t", [128, 2 * 4 * TW], BF16))
            kvq = EA(nc.sbuf_tensor("kvq", [128, 4 * TW], BF16))
            qm_t = EA(nc.sbuf_tensor("qm_t", [64, TW], BF16))
            kpre = EA(nc.sbuf_tensor("kpre", [128, 2 * (TW + 2)], BF16))
            kc1o = EA(nc.sbuf_tensor("kc1o", [128, 2 * (TW + 2)], BF16))
            kf_t = EA(nc.sbuf_tensor("kf_t", [128, 2 * TW], F32))
            kfsq = EA(nc.sbuf_tensor("kfsq", [128, 2 * TW], BF16))
            rn_t = EA(nc.sbuf_tensor("rn_t", [2, 2 * TW], F32))
            krp = EA(nc.sbuf_tensor("krp", [128, 2 * TW], F32))
            ksw = EA(nc.sbuf_tensor("ksw", [128, TW], F32))
            cos_t = EA(nc.sbuf_tensor("cos_t", [128, TW], BF16))
            sin_t = EA(nc.sbuf_tensor("sin_t", [128, TW], BF16))
            vbuf = EA(nc.sbuf_tensor("vbuf", [128, TW + 1], BF16))
            vtr = EA(nc.sbuf_tensor("vtr", [128, 8 * 128], BF16))

            @bk.gpsimd
            def _(g):
                for tap in range(3):
                    for ic in range(2):
                        for oc in range(2):
                            off = (tap * 4 + ic * 2 + oc) * 128
                            i = g.dma_start(out=kcw[:, off:off + 128],
                                            in_=ap["kc1dT"][tap, 128 * ic:128 * (ic + 1),
                                                            128 * oc:128 * (oc + 1)])
                            sG.inc(i)
                            off2 = (12 + tap * 4 + ic * 2 + oc) * 128
                            i = g.dma_start(out=kcw[:, off2:off2 + 128],
                                            in_=ap["kc2dT"][tap, 128 * ic:128 * (ic + 1),
                                                            128 * oc:128 * (oc + 1)])
                            sG.inc(i)
            kcw_ld = sG.n

            @bk.sync
            def _(sy):
                sW.inc(sy.dma_start(out=kc1b_s[:], in_=ap["kc1b"]))
                sW.inc(sy.dma_start(out=kc2b_s[:], in_=ap["kc2b"]))
            kcb_ld = sW.n

            def kc1T(tap, ic, oc):
                o = (tap * 4 + ic * 2 + oc) * 128
                return kcw[:, o:o + 128]

            def kc2T(tap, ic, oc):
                o = (12 + tap * 4 + ic * 2 + oc) * 128
                return kcw[:, o:o + 128]

            rope_done = 0
            tile_pe_done = 0
            k_store = 0
            v_store = 0
            sqrt_prev = 0
            krp_prev = 0
            kv2_done = {}
            qblk_marks = {}
            for tt in range(NT):
                c0 = TW * tt
                bstart = (tt % 4 == 0)
                slot = tt % 2
                xbv = xb_t[:, slot * NF * TW:(slot + 1) * NF * TW]

                sqo_marks = {}

                @bk.gpsimd
                def _(g, tt=tt, c0=c0, xbv=xbv, pe=kv2_done.get(tt - 2, 0),
                      qm=[qblk_marks.get(tt * 8 + b - 2, 0) for b in range(8)],
                      sqm=sqo_marks):
                    if pe:
                        g.wait_ge(sP.h, pe)
                    src = ap["x_t"].rearrange("(c p) t -> p c t", p=128)
                    i = g.dma_start(out=xbv.rearrange("p (c w) -> p c w", c=NF),
                                    in_=src[:, :, c0:c0 + TW])
                    sG.inc(i)
                    g.wait_ge(sG.h, sG.n)
                    for b in range(8):
                        if qm[b]:
                            g.wait_ge(sP.h, qm[b])
                        xs = (xbv.rearrange("p (c w) -> p c w", c=NF)
                              [:, 4 * b:4 * b + 4, :])
                        ds = (sq_t[:, (b % 2) * 2048:(b % 2) * 2048 + 2048]
                              .rearrange("p (c w) -> p c w", c=4))
                        i = g.tensor_mul(ds, xs, xs)
                        sO.inc(i)
                        sqm[b] = sO.n
                    i = g.dma_start(out=ap["wg_bf"][256 * tt:256 * (tt + 1), :],
                                    in_=ap["wgT"][256 * tt:256 * (tt + 1), :])
                    sG.inc(i)
                    i = g.dma_start(out=ap["wu_bf"][256 * tt:256 * (tt + 1), :],
                                    in_=ap["wuT"][256 * tt:256 * (tt + 1), :])
                    sG.inc(i)
                    i = g.dma_start(out=ap["wd_bf"][688 * tt:688 * (tt + 1), :],
                                    in_=ap["wdT"][688 * tt:688 * (tt + 1), :])
                    sG.inc(i)
                xb_ready = sG.n - 48

                @bk.gpsimd
                def _(g, c0=c0, rd=rope_done):
                    if rd:
                        g.wait_ge(sV.h, rd)
                    sG.inc(g.dma_start(out=cos_t[:], in_=ap["cos2"][:, c0:c0 + TW]))
                    sG.inc(g.dma_start(out=sin_t[:], in_=ap["sin2"][:, c0:c0 + TW]))
                trig_ready = sG.n

                @bk.tensor
                def _(t, tt=tt, xbv=xbv, xbr=xb_ready, sqr=dict(sqo_marks),
                      pa=sqrt_prev, pv=krp_prev):
                    t.wait_ge(sG.h, xbr)
                    if tt == 0:
                        t.wait_ge(sG.h, kcw_ld)
                        t.wait_ge(sW.h, kcb_ld)
                    if pv:
                        t.wait_ge(sV.h, pv)
                    if pa:
                        t.wait_ge(sA.h, pa)
                    for fc in range(NF):
                        i = t.matmul(pC[0:1, 0:512], ones_col[:],
                                     xbv[:, TW * fc:TW * (fc + 1)],
                                     start=(fc == 0), stop=(fc == NF - 1))
                    sP.inc(i)
                    for b in range(8):
                        t.wait_ge(sO.h, sqr[b])
                        for f4 in range(4):
                            i = t.matmul(pB[0:1, 0:512], ones_col[:],
                                         sq_t[:, (b % 2) * 2048 + f4 * 512:
                                              (b % 2) * 2048 + f4 * 512 + 512],
                                         start=(b == 0 and f4 == 0),
                                         stop=(b == 7 and f4 == 3))
                        i.then_inc(sP.h, 1)
                        sP.n += 1
                        qblk_marks[tt * 8 + b] = sP.n
                    for ot in range(4):
                        for fc in range(NF):
                            t.matmul(pA[:, 512 * ot:512 * ot + 512],
                                     wkv_sb[:, 512 * fc + 128 * ot:512 * fc + 128 * (ot + 1)],
                                     xbv[:, TW * fc:TW * (fc + 1)],
                                     start=(fc == 0), stop=False)
                s_done = sP.n

                stats_done = stats_math(pC[0:1, 0:TW], pB[0:1, 0:TW], TW, D)

                @bk.tensor
                def _(t, sd=stats_done, xbv=xbv):
                    t.wait_ge(sV.h, sd)
                    t.matmul(pB[:, 512:1024], onesf_row[:], arow[:, 0:TW],
                             start=True, stop=True)
                    for fc in range(NF):
                        t.matmul(pB[0:64, 0:512], qmT[:, 64 * fc:64 * (fc + 1)],
                                 xbv[:, TW * fc:TW * (fc + 1)],
                                 start=(fc == 0), stop=False)
                    for ot in range(4):
                        t.matmul(pA[:, 512 * ot:512 * ot + 512],
                                 w1kv[:, 128 * ot:128 * (ot + 1)], crow[:, 0:TW],
                                 start=False, stop=True)
                    i = t.matmul(pB[0:64, 0:512], w1qm[:], crow[:, 0:TW],
                                 start=False, stop=True)
                    sP.inc(i)
                kv_done = sP.n
                tile_pe_done = kv_done
                kv2_done[tt] = kv_done

                @bk.vector
                def _(v, kvd=kv_done, tt=tt, bstart=bstart):
                    v.wait_ge(sP.h, kvd)
                    v.tensor_copy(a_b[:], pB[:, 512:1024])
                    for ot in range(4):
                        v.tensor_tensor(out=kvq[:, 512 * ot:512 * (ot + 1)],
                                        in0=pA[:, 512 * ot:512 * ot + 512],
                                        in1=a_b[:], op=ALU.mult)
                    v.tensor_tensor(out=qm_t[:], in0=pB[0:64, 0:512],
                                    in1=a_b[0:64, :], op=ALU.mult)
                    for c in range(2):
                        if bstart:
                            v.memset(kpre[:, c * 514:c * 514 + 2], 0.0)
                            v.memset(kc1o[:, c * 514:c * 514 + 2], 0.0)
                        else:
                            v.tensor_copy(kpre[:, c * 514:c * 514 + 2],
                                          kpre[:, c * 514 + 512:c * 514 + 514])
                            v.tensor_copy(kc1o[:, c * 514:c * 514 + 2],
                                          kc1o[:, c * 514 + 512:c * 514 + 514])
                    if bstart:
                        v.memset(vbuf[:, 0:1], 0.0)
                    else:
                        v.tensor_copy(vbuf[:, 0:1], vbuf[:, TW:TW + 1])
                    for c in range(2):
                        v.tensor_copy(kpre[:, c * 514 + 2:c * 514 + 514],
                                      kvq[:, 512 * c:512 * (c + 1)])
                    i = v.tensor_copy(vbuf[:, 1:TW + 1], kvq[:, 512 * 3:512 * 4])
                    sV.inc(i)
                kvq_done = sV.n

                @bk.tensor
                def _(t, kd=kvq_done):
                    t.wait_ge(sV.h, kd)
                    for oc in range(2):
                        first = True
                        for tap in range(3):
                            for ic in range(2):
                                i = t.matmul(pA[:, 512 * oc:512 * oc + 512],
                                             kc1T(tap, ic, oc),
                                             kpre[:, ic * 514 + tap:ic * 514 + tap + 512],
                                             start=first, stop=(tap == 2 and ic == 1))
                                first = False
                    sP.inc(i)
                kc1_done = sP.n

                @bk.scalar
                def _(sc, p=kc1_done, kd=kvq_done):
                    sc.wait_ge(sP.h, p)
                    sc.wait_ge(sV.h, kd)
                    for oc in range(2):
                        i = sc.activation(out=kc1o[:, oc * 514 + 2:oc * 514 + 514],
                                          in_=pA[:, 512 * oc:512 * oc + 512],
                                          func=AF.Identity, bias=kc1b_s[:, oc:oc + 1])
                    sA.inc(i)
                kc1o_done = sA.n

                @bk.tensor
                def _(t, a1=kc1o_done, kd=kvq_done):
                    t.wait_ge(sA.h, a1)
                    for oc in range(2):
                        first = True
                        for tap in range(3):
                            for ic in range(2):
                                i = t.matmul(pA[:, 1024 + 512 * oc:1024 + 512 * oc + 512],
                                             kc2T(tap, ic, oc),
                                             kc1o[:, ic * 514 + tap:ic * 514 + tap + 512],
                                             start=first, stop=(tap == 2 and ic == 1))
                                first = False
                    i = t.matmul(pB[:, 512:1024], sel64_s[:], qm_t[:],
                                 start=True, stop=True)
                    sP.inc(i)
                kc2_done = sP.n

                @bk.scalar
                def _(sc, p=kc2_done):
                    sc.wait_ge(sP.h, p)
                    for oc in range(2):
                        i = sc.activation(out=kf_t[:, TW * oc:TW * (oc + 1)],
                                          in_=pA[:, 1024 + 512 * oc:1024 + 512 * oc + 512],
                                          func=AF.Identity, bias=kc2b_s[:, oc:oc + 1])
                    sA.inc(i)
                kc2o_done = sA.n

                @bk.vector
                def _(v, a1=kc2o_done):
                    v.wait_ge(sA.h, a1)
                    for oc in range(2):
                        i = v.scalar_tensor_tensor(out=kf_t[:, TW * oc:TW * (oc + 1)],
                                                   in0=pB[:, 512:1024], scalar=0.5,
                                                   in1=kf_t[:, TW * oc:TW * (oc + 1)],
                                                   op0=ALU.mult, op1=ALU.add)
                    sV.inc(i)
                kf_done = sV.n

                @bk.scalar
                def _(sc, vv=kf_done):
                    sc.wait_ge(sV.h, vv)
                    i = sc.activation(out=kfsq[:], in_=kf_t[:], func=AF.Square)
                    sA.inc(i)
                kfsq_done = sA.n

                @bk.tensor
                def _(t, a1=kfsq_done):
                    t.wait_ge(sA.h, a1)
                    t.matmul(pC[0:2, 0:512], bdo_s[:], kfsq[:, 0:TW], start=True, stop=True)
                    i = t.matmul(pB[0:2, 0:512], bdo_s[:], kfsq[:, TW:2 * TW],
                                 start=True, stop=True)
                    sP.inc(i)
                n2_done = sP.n

                @bk.scalar
                def _(sc, p=n2_done):
                    sc.wait_ge(sP.h, p)
                    sc.activation(out=rn_t[:, 0:TW], in_=pC[0:2, 0:512], func=AF.Sqrt)
                    i = sc.activation(out=rn_t[:, TW:2 * TW], in_=pB[0:2, 0:512], func=AF.Sqrt)
                    sA.inc(i)
                sqrt_done = sA.n

                @bk.vector
                def _(v, a1=sqrt_done):
                    v.wait_ge(sA.h, a1)
                    v.tensor_scalar(out=rn_t[:], in0=rn_t[:], scalar1=1e-12,
                                    scalar2=None, op0=ALU.max)
                    v.reciprocal(rn_t[:], rn_t[:])
                    i = v.tensor_scalar(out=rn_t[:], in0=rn_t[:], scalar1=kt2[:],
                                        scalar2=None, op0=ALU.mult)
                    sV.inc(i)
                rn_done = sV.n

                @bk.tensor
                def _(t, vv=rn_done):
                    t.wait_ge(sV.h, vv)
                    t.matmul(pB[:, 512:1024], selkf_s[:], rn_t[:, 0:TW],
                             start=True, stop=True)
                    i = t.matmul(pB[:, 0:512], selkf_s[:], rn_t[:, TW:2 * TW],
                                 start=True, stop=True)
                    sP.inc(i)
                rnb_done = sP.n

                @bk.vector
                def _(v, p=rnb_done, ks=k_store):
                    v.wait_ge(sP.h, p)
                    if ks:
                        v.wait_ge(sS.h, ks)
                    v.tensor_tensor(out=krp[:, 0:TW], in0=kf_t[:, 0:TW],
                                    in1=pB[:, 512:1024], op=ALU.mult)
                    i = v.tensor_tensor(out=krp[:, TW:2 * TW], in0=kf_t[:, TW:2 * TW],
                                        in1=pB[:, 0:512], op=ALU.mult)
                    sV.inc(i)
                krp_prev = sV.n
                sqrt_prev = sA.n

                @bk.vector
                def _(v, tr=trig_ready):
                    v.wait_ge(sG.h, tr)
                    for c in range(2):
                        a0 = c * TW
                        for blk in range(2):
                            r0, r1 = 64 * blk, 64 * blk + 32
                            v.tensor_copy(ksw[r0:r0 + 32, :], krp[r1:r1 + 32, a0:a0 + TW])
                            v.tensor_copy(ksw[r1:r1 + 32, :], krp[r0:r0 + 32, a0:a0 + TW])
                        v.tensor_tensor(out=krp[:, a0:a0 + TW], in0=krp[:, a0:a0 + TW],
                                        in1=cos_t[:], op=ALU.mult)
                        v.tensor_tensor(out=ksw[:], in0=ksw[:], in1=sin_t[:], op=ALU.mult)
                        i = v.tensor_tensor(out=krp[:, a0:a0 + TW], in0=krp[:, a0:a0 + TW],
                                            in1=ksw[:], op=ALU.add)
                    sV.inc(i)
                rope_done = sV.n

                @bk.gpsimd
                def _(g, vv=rope_done, c0=c0):
                    g.wait_ge(sV.h, vv)
                    for c in range(2):
                        i = g.dma_start(out=ap["K_dram"][128 * c:128 * (c + 1), c0:c0 + TW],
                                        in_=krp[:, TW * c:TW * (c + 1)])
                        sS.inc(i)
                k_store = sS.n

                @bk.tensor
                def _(t, kd=kvq_done, rnb=rnb_done):
                    t.wait_ge(sV.h, kd)
                    for i_ in range(4):
                        t.transpose(pD[:, 128 * i_:128 * (i_ + 1)],
                                    kvq[:, 1024 + 128 * i_:1024 + 128 * (i_ + 1)], eye_s[:])
                    for i_ in range(4):
                        i = t.transpose(pD[:, 512 + 128 * i_:512 + 128 * (i_ + 1)],
                                        vbuf[:, 128 * i_:128 * i_ + 128], eye_s[:])
                    sP.inc(i)
                vtr_done = sP.n

                @bk.vector
                def _(v, p=vtr_done, vs=v_store):
                    v.wait_ge(sP.h, p)
                    if vs:
                        v.wait_ge(sS.h, vs)
                    i = v.tensor_copy(vtr[:], pD[:])
                    sV.inc(i)
                vtr_evac = sV.n

                @bk.sync
                def _(sy, vv=vtr_evac, c0=c0):
                    sy.wait_ge(sV.h, vv)
                    for c in range(2):
                        for i_ in range(4):
                            i = sy.dma_start(
                                out=ap["V_dram"][c0 + 128 * i_:c0 + 128 * (i_ + 1),
                                                 128 * c:128 * (c + 1)],
                                in_=vtr[:, (c * 4 + i_) * 128:(c * 4 + i_ + 1) * 128])
                            sS.inc(i)
                v_store = sS.n

            barrier()
        stA.close()

        def waitall(e, snap):
            for h, n in snap:
                if n:
                    e.wait_ge(h, n)

        def snap():
            return [(c.h, c.n) for c in CHANS]

        if "B" in phases:
            stB = contextlib.ExitStack()
            EB = stB.enter_context
            sqB = EB(nc.sbuf_tensor("sqB", [128, 4 * 512], BF16))
            qsq = EB(nc.sbuf_tensor("qsq", [128, 1024], BF16))
            qpre = EB(nc.sbuf_tensor("qpre", [128, 8 * 1040], BF16))
            km_t = EB(nc.sbuf_tensor("km_t", [64, 1040], BF16))
            qc1o = EB(nc.sbuf_tensor("qc1o", [128, 8 * 1040], BF16))
            a_bB = EB(nc.sbuf_tensor("a_bB", [128, 3 * 512], F32))
            crowB = EB(nc.sbuf_tensor("crowB", [1, 3 * 512], F32))
            wqB = EB(nc.sbuf_tensor("wqB", [128, NF * 128], BF16))
            qcw = EB(nc.sbuf_tensor("qcw", [128, 24 * 128], BF16))
            qc1b_s = EB(nc.sbuf_tensor("qc1b_s", [128, 8], F32))
            qc2b_s = EB(nc.sbuf_tensor("qc2b_s", [128, 8], F32))
            qtmp = EB(nc.sbuf_tensor("qtmp", [128, 256], F32))
            rnq = EB(nc.sbuf_tensor("rnq", [2, 1024], F32))
            qswp = EB(nc.sbuf_tensor("qswp", [128, 1024], F32))
            cosq = EB(nc.sbuf_tensor("cosq", [128, 1024], F32))
            sinq = EB(nc.sbuf_tensor("sinq", [128, 1024], F32))
            stB1 = contextlib.ExitStack()
            xoh = stB1.enter_context(nc.sbuf_tensor("xoh", [128, NF * 1040], BF16))

            @bk.gpsimd
            def _(g, sn=snap()):
                waitall(g, sn)
                src = ap["x_ownh"].rearrange("(c p) t -> p c t", p=128)
                i = g.dma_start(out=xoh[:].rearrange("p (c w) -> p c w", c=NF), in_=src)
                sG.inc(i)

            @bk.sync
            def _(sy, sn=snap()):
                waitall(sy, sn)
                sW.inc(sy.dma_start(out=qc1b_s[:], in_=ap["qc1b"]))
                sW.inc(sy.dma_start(out=qc2b_s[:], in_=ap["qc2b"]))
                sW.inc(sy.dma_start(out=cosq[:], in_=ap["cos2o"]))
                sW.inc(sy.dma_start(out=sinq[:], in_=ap["sin2o"]))

            CG = [(0, 512), (512, 512), (1024, 16)]
            for icg, (s0, w) in enumerate(CG):
                @bk.tensor
                def _(t, s0=s0, w=w, sn=snap()):
                    waitall(t, sn)
                    for fc in range(NF):
                        i = t.matmul(pC[0:1, 0:w], ones_col[:],
                                     xoh[:, fc * 1040 + s0:fc * 1040 + s0 + w],
                                     start=(fc == 0), stop=(fc == NF - 1))
                    sP.inc(i)

                for blk in range(8):
                    @bk.gpsimd
                    def _(g, s0=s0, w=w, blk=blk, sn=snap()):
                        waitall(g, sn)
                        src = (xoh[:].rearrange("p (c t) -> p c t", c=NF)
                               [:, 4 * blk:4 * blk + 4, s0:s0 + w])
                        dst = sqB[:].rearrange("p (c t) -> p c t", c=4)[:, :, 0:w]
                        i = g.tensor_mul(dst, src, src)
                        sO.inc(i)

                    @bk.tensor
                    def _(t, w=w, blk=blk, sn=snap()):
                        waitall(t, sn)
                        for f4 in range(4):
                            i = t.matmul(pB[0:1, 0:w], ones_col[:],
                                         sqB[:, f4 * 512:f4 * 512 + w],
                                         start=(blk == 0 and f4 == 0),
                                         stop=(blk == 7 and f4 == 3))
                        sP.inc(i)

                sd = stats_math(pC[0:1, 0:w], pB[0:1, 0:w], w, D)

                @bk.tensor
                def _(t, w=w, sd=sd):
                    t.wait_ge(sV.h, sd)
                    i = t.matmul(pB[:, 512:512 + w], onesf_row[:], arow[:, 0:w],
                                 start=True, stop=True)
                    sP.inc(i)

                @bk.vector
                def _(v, icg=icg, w=w, sn=snap()):
                    waitall(v, sn)
                    v.tensor_copy(a_bB[:, icg * 512:icg * 512 + w], pB[:, 512:512 + w])
                    i = v.tensor_copy(crowB[:, icg * 512:icg * 512 + w], crow[:, 0:w])
                    sV.inc(i)

            for ot in range(9):
                slot = ot % 2

                @bk.gpsimd
                def _(g, ot=ot, sn=snap()):
                    waitall(g, sn)
                    if ot < 8:
                        src = ap["wa_qT"].rearrange("(c p) o -> p c o", p=128)
                        i = g.dma_start(
                            out=wqB[:].rearrange("p (c o) -> p c o", c=NF),
                            in_=src[:, :, 128 * ot:128 * (ot + 1)])
                        sG.inc(i)

                for icg, (s0, w) in enumerate(CG):
                    @bk.tensor
                    def _(t, ot=ot, icg=icg, s0=s0, w=w, sn=snap()):
                        waitall(t, sn)
                        bank = pA[:, 1024 * (icg % 2):1024 * (icg % 2) + w]
                        for fc in range(NF):
                            lhs = (wqB[:, fc * 128:fc * 128 + 128]
                                   if ot < 8 else kmT[:, 64 * fc:64 * (fc + 1)])
                            t.matmul(bank[0:128 if ot < 8 else 64, :], lhs,
                                     xoh[:, fc * 1040 + s0:fc * 1040 + s0 + w],
                                     start=(fc == 0), stop=False)
                        w1s = (w1q[:, 128 * ot:128 * (ot + 1)] if ot < 8 else w1km[:])
                        i = t.matmul(bank[0:128 if ot < 8 else 64, :], w1s,
                                     crowB[:, icg * 512:icg * 512 + w],
                                     start=False, stop=True)
                        sP.inc(i)

                    @bk.vector
                    def _(v, ot=ot, icg=icg, s0=s0, w=w, sn=snap()):
                        waitall(v, sn)
                        bank = pA[:, 1024 * (icg % 2):1024 * (icg % 2) + w]
                        ab = a_bB[:, icg * 512:icg * 512 + w]
                        if ot < 8:
                            i = v.tensor_tensor(out=qpre[:, ot * 1040 + s0:ot * 1040 + s0 + w],
                                                in0=bank, in1=ab, op=ALU.mult)
                        else:
                            i = v.tensor_tensor(out=km_t[:, s0:s0 + w],
                                                in0=bank[0:64, :], in1=ab[0:64, :],
                                                op=ALU.mult)
                        sV.inc(i)

            barrier()
            stB1.close()
            qf = EB(nc.sbuf_tensor("qf", [128, 8 * 1024], F32))

            for oc in range(8):
                slot = oc % 2

                @bk.gpsimd
                def _(g, oc=oc, sn=snap()):
                    waitall(g, sn)
                    for tap in range(3):
                        src = ap["qc1dT"][tap].rearrange("(c p) o -> p c o", p=128)
                        i = g.dma_start(
                            out=qcw[:, (tap * 8) * 128:(tap * 8 + 8) * 128]
                                .rearrange("p (c o) -> p c o", c=8),
                            in_=src[:, :, 128 * oc:128 * (oc + 1)])
                        sG.inc(i)

                for b_ in range(4):
                    @bk.tensor
                    def _(t, oc=oc, b_=b_, sn=snap()):
                        waitall(t, sn)
                        bank = pA[:, 1024 * (b_ % 2):1024 * (b_ % 2) + 258]
                        first = True
                        for tap in range(3):
                            for ic in range(8):
                                lhs = qcw[:, (tap * 8 + ic) * 128:
                                          (tap * 8 + ic + 1) * 128]
                                i = t.matmul(bank, lhs,
                                             qpre[:, ic * 1040 + b_ * 260 + tap:
                                                  ic * 1040 + b_ * 260 + tap + 258],
                                             start=first, stop=(tap == 2 and ic == 7))
                                first = False
                        sP.inc(i)

                    @bk.scalar
                    def _(sc, oc=oc, b_=b_, sn=snap()):
                        waitall(sc, sn)
                        bank = pA[:, 1024 * (b_ % 2):1024 * (b_ % 2) + 258]
                        i = sc.activation(out=qc1o[:, oc * 1040 + b_ * 260 + 2:
                                                   oc * 1040 + b_ * 260 + 260],
                                          in_=bank, func=AF.Identity,
                                          bias=qc1b_s[:, oc:oc + 1])
                        sA.inc(i)

            for b_ in range(4):
                @bk.tensor
                def _(t, b_=b_, sn=snap()):
                    waitall(t, sn)
                    i = t.matmul(pB[:, 512:768], sel64_s[:],
                                 km_t[:, b_ * 260 + 4:b_ * 260 + 260],
                                 start=True, stop=True)
                    sP.inc(i)
                kmb_p = sP.n

                for oc in range(8):
                    slot = oc % 2

                    @bk.gpsimd
                    def _(g, oc=oc, sn=snap()):
                        waitall(g, sn)
                        for tap in range(3):
                            src = ap["qc2dT"][tap].rearrange("(c p) o -> p c o", p=128)
                            i = g.dma_start(
                                out=qcw[:, (tap * 8) * 128:(tap * 8 + 8) * 128]
                                    .rearrange("p (c o) -> p c o", c=8),
                                in_=src[:, :, 128 * oc:128 * (oc + 1)])
                            sG.inc(i)

                    @bk.tensor
                    def _(t, oc=oc, b_=b_, sn=snap()):
                        waitall(t, sn)
                        bank = pA[:, 1024 * (oc % 2):1024 * (oc % 2) + 256]
                        first = True
                        for tap in range(3):
                            for ic in range(8):
                                lhs = qcw[:, (tap * 8 + ic) * 128:
                                          (tap * 8 + ic + 1) * 128]
                                i = t.matmul(bank, lhs,
                                             qc1o[:, ic * 1040 + b_ * 260 + 2 + tap:
                                                  ic * 1040 + b_ * 260 + tap + 258],
                                             start=first, stop=(tap == 2 and ic == 7))
                                first = False
                        sP.inc(i)

                    @bk.scalar
                    def _(sc, oc=oc, b_=b_, sn=snap()):
                        waitall(sc, sn)
                        bank = pA[:, 1024 * (oc % 2):1024 * (oc % 2) + 256]
                        i = sc.activation(out=qtmp[:], in_=bank, func=AF.Identity,
                                          bias=qc2b_s[:, oc:oc + 1])
                        sA.inc(i)

                    @bk.vector
                    def _(v, oc=oc, b_=b_, sn=snap(), kp=kmb_p):
                        waitall(v, sn)
                        v.wait_ge(sP.h, kp)
                        i = v.scalar_tensor_tensor(
                            out=qf[:, oc * 1024 + b_ * 256:oc * 1024 + b_ * 256 + 256],
                            in0=pB[:, 512:768], scalar=0.5, in1=qtmp[:],
                            op0=ALU.mult, op1=ALU.add)
                        sV.inc(i)

            for oc in range(8):
                @bk.scalar
                def _(sc, oc=oc, sn=snap()):
                    waitall(sc, sn)
                    i = sc.activation(out=qsq[:], in_=qf[:, oc * 1024:(oc + 1) * 1024],
                                      func=AF.Square)
                    sA.inc(i)

                @bk.tensor
                def _(t, sn=snap()):
                    waitall(t, sn)
                    t.matmul(pC[0:2, 0:512], bdo_s[:], qsq[:, 0:512], start=True, stop=True)
                    i = t.matmul(pB[0:2, 0:512], bdo_s[:], qsq[:, 512:1024],
                                 start=True, stop=True)
                    sP.inc(i)

                @bk.scalar
                def _(sc, sn=snap()):
                    waitall(sc, sn)
                    sc.activation(out=rnq[:, 0:512], in_=pC[0:2, 0:512], func=AF.Sqrt)
                    i = sc.activation(out=rnq[:, 512:1024], in_=pB[0:2, 0:512], func=AF.Sqrt)
                    sA.inc(i)

                @bk.vector
                def _(v, sn=snap()):
                    waitall(v, sn)
                    v.tensor_scalar(out=rnq[:], in0=rnq[:], scalar1=1e-12,
                                    scalar2=None, op0=ALU.max)
                    i = v.reciprocal(rnq[:], rnq[:])
                    sV.inc(i)

                @bk.tensor
                def _(t, sn=snap()):
                    waitall(t, sn)
                    t.matmul(pB[:, 512:1024], selkf_s[:], rnq[:, 0:512], start=True, stop=True)
                    i = t.matmul(pC[:, 0:512], selkf_s[:], rnq[:, 512:1024],
                                 start=True, stop=True)
                    sP.inc(i)

                @bk.vector
                def _(v, oc=oc, sn=snap()):
                    waitall(v, sn)
                    v.tensor_tensor(out=qf[:, oc * 1024:oc * 1024 + 512],
                                    in0=qf[:, oc * 1024:oc * 1024 + 512],
                                    in1=pB[:, 512:1024], op=ALU.mult)
                    v.tensor_tensor(out=qf[:, oc * 1024 + 512:oc * 1024 + 1024],
                                    in0=qf[:, oc * 1024 + 512:oc * 1024 + 1024],
                                    in1=pC[:, 0:512], op=ALU.mult)
                    qslice = qf[:, oc * 1024:(oc + 1) * 1024]
                    for blk in range(2):
                        r0, r1 = 64 * blk, 64 * blk + 32
                        v.tensor_copy(qswp[r0:r0 + 32, :], qslice[r1:r1 + 32, :])
                        v.tensor_copy(qswp[r1:r1 + 32, :], qslice[r0:r0 + 32, :])
                    v.tensor_tensor(out=qslice, in0=qslice, in1=cosq[:], op=ALU.mult)
                    v.tensor_tensor(out=qswp[:], in0=qswp[:], in1=sinq[:], op=ALU.mult)
                    i = v.tensor_tensor(out=Q[:, oc * 1024:(oc + 1) * 1024],
                                        in0=qslice, in1=qswp[:], op=ALU.add)
                    sV.inc(i)

            barrier()
            stB.close()
        stQM.close()

        if "C" in phases:
            stC = contextlib.ExitStack()
            EC = stC.enter_context
            AO = EC(nc.sbuf_tensor("AO", [128, 8 * 1024], BF16))
            PT = EC(nc.sbuf_tensor("PT", [128, 2 * 16 * 256], BF16))
            mask_s = EC(nc.sbuf_tensor("mask_s", [128, 16 * 256], BF16))
            ksl = EC(nc.sbuf_tensor("ksl", [128, 2 * 2048], BF16))
            vsl = EC(nc.sbuf_tensor("vsl", [128, 2 * 16 * 64], BF16))
            rlrow = EC(nc.sbuf_tensor("rlrow", [1, 2 * 256], F32))
            ao_t = EC(nc.sbuf_tensor("ao_t", [64, 256], F32))
            woB = EC(nc.sbuf_tensor("woB", [128, 2 * 8 * 128], BF16))
            xrB = EC(nc.sbuf_tensor("xrB", [128, 2 * 512], F32))
            xoB = EC(nc.sbuf_tensor("xoB", [128, 2 * 512], F32))

            @bk.sync
            def _(sy, sn=snap()):
                waitall(sy, sn)
                sW.inc(sy.dma_start(out=mask_s[:], in_=ap["mask"]))

            units = [(b_, g_, hh) for b_ in range(B) for g_ in range(HKV)
                     for hh in range(4)]
            ld_mark = {}
            pe_grp_mark = {}
            expw = {}
            mask_m = {}
            lav_m = {}
            recip_m = {}
            rlb_m = {}
            ao_m = {}
            ngrp = len(units) // 4

            def emit_group_load(grp, first=False):
                b_, g_ = units[grp * 4][0], units[grp * 4][1]
                gsl = grp % 2

                @bk.sync
                def _(sy, b_=b_, g_=g_, gsl=gsl,
                      pm=lav_m.get(4 * (grp - 1) - 1, 0) if not first else 0,
                      pm2=pe_grp_mark.get(grp - 2, 0),
                      sn=(snap() if first else None)):
                    if sn is not None:
                        waitall(sy, sn)
                    if pm2:
                        sy.wait_ge(sP.h, pm2)
                    if pm:
                        sy.wait_ge(sP.h, pm)
                    r0 = 128 * (g_ // 2) + 64 * (g_ % 2)
                    i = sy.dma_start(out=ksl[0:64, gsl * 2048:(gsl + 1) * 2048],
                                     in_=ap["K_dram"][r0:r0 + 64,
                                                      b_ * 2048:(b_ + 1) * 2048])
                    sW.inc(i)
                    i = sy.dma_start(out=ksl[64:128, gsl * 2048:(gsl + 1) * 2048],
                                     in_=ap["K_dram"][r0:r0 + 64,
                                                      b_ * 2048:(b_ + 1) * 2048])
                    sW.inc(i)
                    srcv = ap["V_dram"].rearrange("(i p) d -> p i d", p=128)
                    i = sy.dma_start(out=vsl[:, gsl * 1024:(gsl + 1) * 1024]
                                     .rearrange("p (i d) -> p i d", i=16),
                                     in_=srcv[:, 16 * b_:16 * (b_ + 1),
                                              64 * g_:64 * (g_ + 1)])
                    sW.inc(i)
                ld_mark[grp] = sW.n

            emit_group_load(0, first=True)

            def emit_lav(u):
                b_, g_, hh = units[u]
                gsl = (u // 4) % 2
                psl = u % 2
                av0 = 256 + 512 * (u % 2)
                rl0 = 512 * (u % 2)

                @bk.tensor
                def _(t, u=u, gsl=gsl, psl=psl, av0=av0, mm=mask_m[u],
                      aom=ao_m.get(u - 2, 0), rcm=recip_m.get(u - 1, 0)):
                    t.wait_ge(sV.h, mm)
                    if aom:
                        t.wait_ge(sV.h, aom)
                    if rcm:
                        t.wait_ge(sV.h, rcm)
                    for tk in range(16):
                        i = t.matmul(pC[0:1, 0:256], ones_col[:],
                                     PT[:, psl * 4096 + 256 * tk:psl * 4096 + 256 * (tk + 1)],
                                     start=(tk == 0), stop=(tk == 15))
                    for tk in range(16):
                        i = t.matmul(pB[0:64, av0:av0 + 256],
                                     vsl[:, gsl * 1024 + 64 * tk:gsl * 1024 + 64 * (tk + 1)],
                                     PT[:, psl * 4096 + 256 * tk:psl * 4096 + 256 * (tk + 1)],
                                     start=(tk == 0), stop=(tk == 15))
                    sP.inc(i)
                lav_m[u] = sP.n

                @bk.vector
                def _(v, u=u, p=lav_m[u]):
                    v.wait_ge(sP.h, p)
                    i = v.reciprocal(rlrow[:, 256 * (u % 2):256 * (u % 2) + 256],
                                     pC[0:1, 0:256])
                    sV.inc(i)
                recip_m[u] = sV.n

            def emit_rlb(u):
                av0 = 256 + 512 * (u % 2)
                rl0 = 512 * (u % 2)

                @bk.tensor
                def _(t, u=u, rl0=rl0, rm=recip_m[u], aom=ao_m.get(u - 2, 0)):
                    t.wait_ge(sV.h, rm)
                    if aom:
                        t.wait_ge(sV.h, aom)
                    i = t.matmul(pB[0:64, rl0:rl0 + 256], onesf_row[:, 0:64],
                                 rlrow[:, 256 * (u % 2):256 * (u % 2) + 256],
                                 start=True, stop=True)
                    sP.inc(i)
                rlb_m[u] = sP.n

                b_, g_, hh = units[u]
                h = 4 * g_ + hh

                @bk.vector
                def _(v, h=h, b_=b_, av0=av0, rl0=rl0, p=rlb_m[u]):
                    v.wait_ge(sP.h, p)
                    v.tensor_copy(ao_t[:], pB[0:64, av0:av0 + 256])
                    dst = AO[64 * (h % 2):64 * (h % 2) + 64,
                             (h // 2) * 1024 + b_ * 256:(h // 2) * 1024 + b_ * 256 + 256]
                    i = v.tensor_tensor(out=dst, in0=ao_t[:], in1=pB[0:64, rl0:rl0 + 256],
                                        op=ALU.mult)
                    sV.inc(i)
                ao_m[u] = sV.n

            for u in range(len(units)):
                b_, g_, hh = units[u]
                h = 4 * g_ + hh
                grp = u // 4
                gsl = grp % 2
                psl = u % 2
                qrhs = Q[64 * (h % 2):64 * (h % 2) + 64,
                         (h // 2) * 1024 + b_ * 256:(h // 2) * 1024 + b_ * 256 + 256]

                for wv in range(2):
                    @bk.tensor
                    def _(t, u=u, wv=wv, gsl=gsl, h=h, qrhs=qrhs,
                          ldm=ld_mark[grp],
                          pa_free=(expw.get((u - 1, 1), 0) if wv == 0
                                   else expw.get((u, 0), 0))):
                        t.wait_ge(sW.h, ldm)
                        if pa_free:
                            t.wait_ge(sA.h, pa_free)
                        hb = 64 * (h % 2)
                        for i_ in range(8):
                            tk = 8 * wv + i_
                            i = t.matmul(pA[:, 256 * i_:256 * (i_ + 1)],
                                         ksl[hb:hb + 64,
                                             gsl * 2048 + 128 * tk:gsl * 2048 + 128 * (tk + 1)],
                                         qrhs, start=True, stop=True)
                        sP.inc(i)
                    wv_p = sP.n
                    if wv == 1:
                        pe_grp_mark[grp] = wv_p

                    @bk.scalar
                    def _(sc, u=u, wv=wv, psl=psl, p=wv_p,
                          ptfree=lav_m.get(u - 2, 0)):
                        sc.wait_ge(sP.h, p)
                        if ptfree:
                            sc.wait_ge(sP.h, ptfree)
                        i = sc.activation(out=PT[:, psl * 4096 + 2048 * wv:
                                                 psl * 4096 + 2048 * (wv + 1)],
                                          in_=pA[:, 0:2048], func=AF.Exp,
                                          scale=0.125, bias=negB[:])
                        sA.inc(i)
                    expw[(u, wv)] = sA.n

                    if wv == 0 and u >= 1 and (u - 1) not in lav_m:
                        emit_lav(u - 1)

                @bk.vector
                def _(v, u=u, psl=psl, e0=expw[(u, 0)], e1=expw[(u, 1)]):
                    v.wait_ge(sA.h, e1)
                    i = v.tensor_tensor(out=PT[:, psl * 4096:(psl + 1) * 4096],
                                        in0=PT[:, psl * 4096:(psl + 1) * 4096],
                                        in1=mask_s[:], op=ALU.mult)
                    sV.inc(i)
                mask_m[u] = sV.n

                if u >= 2:
                    emit_rlb(u - 2)
                if u % 4 == 0 and u // 4 + 1 < ngrp:
                    emit_group_load(u // 4 + 1)

            emit_lav(len(units) - 1)
            emit_rlb(len(units) - 2)
            emit_rlb(len(units) - 1)

            for oc in range(NF):
                slot = oc % 2

                @bk.gpsimd
                def _(g, oc=oc, slot=slot, sn=snap()):
                    waitall(g, sn)
                    src = ap["wa_oT"].rearrange("(c p) o -> p c o", p=128)
                    i = g.dma_start(
                        out=woB[:, slot * 1024:(slot + 1) * 1024]
                            .rearrange("p (c o) -> p c o", c=8),
                        in_=src[:, :, 128 * oc:128 * (oc + 1)])
                    sG.inc(i)

                @bk.sync
                def _(sy, oc=oc, slot=slot, sn=snap()):
                    waitall(sy, sn)
                    i = sy.dma_start(out=xrB[:, slot * 512:slot * 512 + 512],
                                     in_=ap["x_own"][128 * oc:128 * (oc + 1), 0:512])
                    sW.inc(i)
                    i = sy.dma_start(out=xoB[:, slot * 512:slot * 512 + 512],
                                     in_=ap["x_own"][128 * oc:128 * (oc + 1), 512:1024])
                    sW.inc(i)

                for cg in range(2):
                    @bk.tensor
                    def _(t, oc=oc, slot=slot, cg=cg, sn=snap()):
                        waitall(t, sn)
                        bank = pA[:, 1024 * cg:1024 * cg + 512]
                        for lc in range(8):
                            i = t.matmul(bank,
                                         woB[:, slot * 1024 + lc * 128:slot * 1024 + lc * 128 + 128],
                                         AO[:, lc * 1024 + 512 * cg:lc * 1024 + 512 * cg + 512],
                                         start=(lc == 0), stop=(lc == 7))
                        sP.inc(i)

                    @bk.vector
                    def _(v, oc=oc, slot=slot, cg=cg, sn=snap()):
                        waitall(v, sn)
                        bank = pA[:, 1024 * cg:1024 * cg + 512]
                        res = (xrB if cg == 0 else xoB)[:, slot * 512:slot * 512 + 512]
                        i = v.tensor_tensor(out=(xrB if cg == 0 else xoB)
                                            [:, slot * 512:slot * 512 + 512],
                                            in0=bank, in1=res, op=ALU.add)
                        sV.inc(i)

                    @bk.sync
                    def _(sy, oc=oc, slot=slot, cg=cg, sn=snap()):
                        waitall(sy, sn)
                        i = sy.dma_start(out=ap["x1_dram"][128 * oc:128 * (oc + 1),
                                                           512 * cg:512 * cg + 512],
                                         in_=(xrB if cg == 0 else xoB)
                                         [:, slot * 512:slot * 512 + 512])
                        sS.inc(i)

            barrier()
            stC.close()
        stBC.close()

        def ffn_phase(xin, xout):
            stD = contextlib.ExitStack()
            ED = stD.enter_context
            xln = ED(nc.sbuf_tensor("xln" + xin, [128, NF * 1024], BF16))
            wgu = ED(nc.sbuf_tensor("wgu" + xin, [128, 3 * NF * 128], BF16))
            g_s = ED(nc.sbuf_tensor("g_s" + xin, [128, 2 * 512], BF16))
            u_s = ED(nc.sbuf_tensor("u_s" + xin, [128, 512], BF16))
            sgT_s = ED(nc.sbuf_tensor("sgT_s" + xin, [128, NI], F32))
            suT_s = ED(nc.sbuf_tensor("suT_s" + xin, [128, NI], F32))
            sdT_s = ED(nc.sbuf_tensor("sdT_s" + xin, [128, NF], F32))
            wdB = ED(nc.sbuf_tensor("wdB" + xin, [128, 2 * 4 * 512], BF16))
            xrD = ED(nc.sbuf_tensor("xrD" + xin, [128, 2 * 512], F32))
            xtmp = ED(nc.sbuf_tensor("xtmp" + xin, [128, 512], F32))
            stD1 = contextlib.ExitStack()
            xb1 = stD1.enter_context(nc.sbuf_tensor("xb1" + xin, [128, NF * 1024], BF16))
            sqD = stD1.enter_context(nc.sbuf_tensor("sqD" + xin, [128, 4 * 512], BF16))

            @bk.sync
            def _(sy, sn=snap()):
                waitall(sy, sn)
                sW.inc(sy.dma_start(out=sgT_s[:], in_=ap["sgT"]))
                sW.inc(sy.dma_start(out=suT_s[:], in_=ap["suT"]))
                sW.inc(sy.dma_start(out=sdT_s[:], in_=ap["sdT"]))

            @bk.gpsimd
            def _(g, sn=snap()):
                waitall(g, sn)
                src = ap[xin].rearrange("(c p) t -> p c t", p=128)
                i = g.dma_start(out=xb1[:].rearrange("p (c w) -> p c w", c=NF), in_=src)
                sG.inc(i)

            for cg in range(2):
                s0 = 512 * cg

                @bk.tensor
                def _(t, s0=s0, sn=snap()):
                    waitall(t, sn)
                    for fc in range(NF):
                        i = t.matmul(pC[0:1, 0:512], ones_col[:],
                                     xb1[:, fc * 1024 + s0:fc * 1024 + s0 + 512],
                                     start=(fc == 0), stop=(fc == NF - 1))
                    sP.inc(i)

                for blk in range(8):
                    @bk.gpsimd
                    def _(g, s0=s0, blk=blk, sn=snap()):
                        waitall(g, sn)
                        src = (xb1[:].rearrange("p (c t) -> p c t", c=NF)
                               [:, 4 * blk:4 * blk + 4, s0:s0 + 512])
                        dst = sqD[:].rearrange("p (c t) -> p c t", c=4)
                        i = g.tensor_mul(dst, src, src)
                        sO.inc(i)

                    @bk.tensor
                    def _(t, blk=blk, sn=snap()):
                        waitall(t, sn)
                        for f4 in range(4):
                            i = t.matmul(pB[0:1, 0:512], ones_col[:],
                                         sqD[:, f4 * 512:(f4 + 1) * 512],
                                         start=(blk == 0 and f4 == 0),
                                         stop=(blk == 7 and f4 == 3))
                        sP.inc(i)

                sd = stats_math(pC[0:1, 0:512], pB[0:1, 0:512], 512, D)

                @bk.tensor
                def _(t, sd=sd):
                    t.wait_ge(sV.h, sd)
                    t.matmul(pB[:, 512:1024], onesf_row[:], arow[:, 0:512],
                             start=True, stop=True)
                    i = t.matmul(pA[:, 0:512], onesf_neg[:], mrow[:, 0:512],
                                 start=True, stop=True)
                    sP.inc(i)

                @bk.vector
                def _(v, sn=snap()):
                    waitall(v, sn)
                    v.tensor_copy(a_b[:], pB[:, 512:1024])
                    i = v.tensor_copy(nm_b[:], pA[:, 0:512])
                    sV.inc(i)

                for fc in range(NF):
                    @bk.vector
                    def _(v, fc=fc, s0=s0, sn=snap()):
                        waitall(v, sn)
                        xs = xb1[:, fc * 1024 + s0:fc * 1024 + s0 + 512]
                        v.tensor_tensor(out=xtmp[:], in0=xs, in1=nm_b[:], op=ALU.add)
                        i = v.tensor_tensor(out=xln[:, fc * 1024 + s0:fc * 1024 + s0 + 512],
                                            in0=xtmp[:], in1=a_b[:], op=ALU.mult)
                        sV.inc(i)

            barrier()
            stD1.close()
            H = ED(nc.sbuf_tensor("H" + xin, [128, NI * 512], BF16))

            wslot_mark = {}
            silu_mark = {}
            us_mark = {}
            wdslot_mark = {}
            ev_mark = 0
            st_mark = 0
            for hf in range(2):
                t0c = 512 * hf
                for ic in range(NI):
                    sg_ = (2 * ic) % 3
                    su_ = (2 * ic + 1) % 3

                    @bk.gpsimd
                    def _(g, ic=ic, sg_=sg_, su_=su_,
                          m1=wslot_mark.get((2 * ic) % 3, 0),
                          m2=wslot_mark.get((2 * ic + 1) % 3, 0),
                          sn=(snap() if ic == 0 else None)):
                        if sn is not None:
                            waitall(g, sn)
                        if m1:
                            g.wait_ge(sP.h, m1)
                        src = ap["wg_bf"].rearrange("(c p) i -> p c i", p=128)
                        i = g.dma_start(
                            out=wgu[:, sg_ * 4096:(sg_ + 1) * 4096]
                                .rearrange("p (c o) -> p c o", c=NF),
                            in_=src[:, :, 128 * ic:128 * (ic + 1)])
                        sG.inc(i)
                        if m2:
                            g.wait_ge(sP.h, m2)
                        src = ap["wu_bf"].rearrange("(c p) i -> p c i", p=128)
                        i = g.dma_start(
                            out=wgu[:, su_ * 4096:(su_ + 1) * 4096]
                                .rearrange("p (c o) -> p c o", c=NF),
                            in_=src[:, :, 128 * ic:128 * (ic + 1)])
                        sG.inc(i)
                    ld_g = sG.n - 16
                    ld_u = sG.n

                    @bk.tensor
                    def _(t, ic=ic, sg_=sg_, su_=su_, t0c=t0c, ldg=ld_g, ldu=ld_u,
                          sm=silu_mark.get((hf, ic - 2), 0) or silu_mark.get((hf - 1, NI - 2 + ic), 0),
                          um=us_mark.get((hf, ic - 1), 0) or us_mark.get((hf - 1, NI - 1), 0)):
                        t.wait_ge(sG.h, ldg)
                        if sm:
                            t.wait_ge(sA.h, sm)
                        bank = pB[:, 512 * (ic % 2):512 * (ic % 2) + 512]
                        for fc in range(NF):
                            i = t.matmul(bank, wgu[:, sg_ * 4096 + fc * 128:sg_ * 4096 + fc * 128 + 128],
                                         xln[:, fc * 1024 + t0c:fc * 1024 + t0c + 512],
                                         start=(fc == 0), stop=(fc == NF - 1))
                        sP.inc(i)
                        t.wait_ge(sG.h, ldu)
                        if um:
                            t.wait_ge(sV.h, um)
                        for fc in range(NF):
                            i = t.matmul(pC[:], wgu[:, su_ * 4096 + fc * 128:su_ * 4096 + fc * 128 + 128],
                                         xln[:, fc * 1024 + t0c:fc * 1024 + t0c + 512],
                                         start=(fc == 0), stop=(fc == NF - 1))
                        sP.inc(i)
                    g_done, u_done = sP.n - 1, sP.n
                    wslot_mark[sg_] = g_done
                    wslot_mark[su_] = u_done

                    @bk.scalar
                    def _(sc, ic=ic, gd=g_done):
                        sc.wait_ge(sP.h, gd)
                        bank = pB[:, 512 * (ic % 2):512 * (ic % 2) + 512]
                        i = sc.activation(out=g_s[:, 512 * (ic % 2):512 * (ic % 2) + 512],
                                          in_=bank, func=AF.Silu,
                                          scale=sgT_s[:, ic:ic + 1])
                        sA.inc(i)
                    silu_mark[(hf, ic)] = sA.n

                    @bk.vector
                    def _(v, ic=ic, ud=u_done, sm=silu_mark[(hf, ic)]):
                        v.wait_ge(sP.h, ud)
                        v.tensor_scalar(out=u_s[:], in0=pC[:], scalar1=suT_s[:, ic:ic + 1],
                                        scalar2=None, op0=ALU.mult)
                        v.wait_ge(sA.h, sm)
                        i = v.tensor_tensor(out=H[:, 512 * ic:512 * (ic + 1)],
                                            in0=g_s[:, 512 * (ic % 2):512 * (ic % 2) + 512],
                                            in1=u_s[:], op=ALU.mult)
                        sV.inc(i)
                    us_mark[(hf, ic)] = sV.n

                @bk.tensor
                def _(t, sn=snap()):
                    waitall(t, sn)
                nblk = (NI + 3) // 4
                for ob in range(8):
                    for blki in range(nblk):
                        i0 = 4 * blki
                        ni = min(4, NI - i0)
                        slot = blki % 2

                        @bk.gpsimd
                        def _(g, i0=i0, ni=ni, slot=slot, ob=ob,
                              m=wdslot_mark.get(slot, 0)):
                            if m:
                                g.wait_ge(sP.h, m)
                            src = ap["wd_bf"].rearrange("(c p) o -> p c o", p=128)
                            i = g.dma_start(
                                out=wdB[:, slot * 2048:slot * 2048 + ni * 512]
                                    .rearrange("p (c o) -> p c o", c=ni),
                                in_=src[:, i0:i0 + ni, 512 * ob:512 * (ob + 1)])
                            sG.inc(i)
                        ldd = sG.n

                        @bk.tensor
                        def _(t, i0=i0, ni=ni, slot=slot, t0c=t0c, ldd=ldd,
                              em=ev_mark, blki=blki):
                            t.wait_ge(sG.h, ldd)
                            if blki == 0 and em:
                                t.wait_ge(sV.h, em)
                            for k in range(ni):
                                ic = i0 + k
                                for oq in range(4):
                                    i = t.matmul(
                                        pA[:, 512 * oq:512 * oq + 512],
                                        wdB[:, slot * 2048 + k * 512 + 128 * oq:
                                            slot * 2048 + k * 512 + 128 * (oq + 1)],
                                        H[:, 512 * ic:512 * (ic + 1)],
                                        start=(ic == 0), stop=(ic == NI - 1))
                            sP.inc(i)
                        wdslot_mark[slot] = sP.n
                    y_done = sP.n

                    for rnd in range(2):
                        @bk.sync
                        def _(sy, ob=ob, t0c=t0c, rnd=rnd, stm=st_mark):
                            if stm:
                                sy.wait_ge(sS.h, stm)
                            for oq2 in range(2):
                                oc = 4 * ob + 2 * rnd + oq2
                                i = sy.dma_start(out=xrD[:, 512 * oq2:512 * oq2 + 512],
                                                 in_=ap[xin][128 * oc:128 * (oc + 1),
                                                             t0c:t0c + 512])
                                sW.inc(i)
                        xr_ld = sW.n

                        @bk.vector
                        def _(v, ob=ob, rnd=rnd, yd=y_done, xl_=xr_ld):
                            v.wait_ge(sP.h, yd)
                            v.wait_ge(sW.h, xl_)
                            for oq2 in range(2):
                                oq = 2 * rnd + oq2
                                oc = 4 * ob + oq
                                i = v.scalar_tensor_tensor(
                                    out=xrD[:, 512 * oq2:512 * oq2 + 512],
                                    in0=pA[:, 512 * oq:512 * oq + 512],
                                    scalar=sdT_s[:, oc:oc + 1],
                                    in1=xrD[:, 512 * oq2:512 * oq2 + 512],
                                    op0=ALU.mult, op1=ALU.add)
                            sV.inc(i)
                        ev_mark = sV.n

                        @bk.sync
                        def _(sy, ob=ob, rnd=rnd, t0c=t0c, em=ev_mark):
                            sy.wait_ge(sV.h, em)
                            for oq2 in range(2):
                                oc = 4 * ob + 2 * rnd + oq2
                                i = sy.dma_start(out=ap[xout][128 * oc:128 * (oc + 1),
                                                              t0c:t0c + 512],
                                                 in_=xrD[:, 512 * oq2:512 * oq2 + 512])
                                sS.inc(i)
                        st_mark = sS.n
            barrier()
            stD.close()

        for nm, scrname in [("INIT1", "x1_dram"), ("INIT2", "x2_dram"),
                            ("INIT3", "x3_dram")]:
            if nm in phases:
                inp("xinit_" + scrname, [D, OWNT], F32)

                @bk.sync
                def _(sy, scrname=scrname, sn=snap()):
                    waitall(sy, sn)
                    i = sy.dma_start(out=ap[scrname], in_=ap["xinit_" + scrname])
                    sS.inc(i)
                barrier()

        if "D1" in phases:
            ffn_phase("x1_dram", "x2_dram")

        doEm = any(x in phases for x in ("E", "Em", "Ep"))
        doEp = any(x in phases for x in ("E", "Ep"))
        if any(x in phases for x in ("E", "Eq", "Em", "Ep")):
            stE = contextlib.ExitStack()
            EE = stE.enter_context
            qb = EE(nc.sbuf_tensor("qb", [128, 48 * 1024], BF16))
            stE1 = contextlib.ExitStack()
            E1 = stE1.enter_context
            wqbB = E1(nc.sbuf_tensor("wqbB", [128, 2 * NF * 128], BF16))
            a_bE = E1(nc.sbuf_tensor("a_bE", [128, 2 * 512], F32))
            crowE = E1(nc.sbuf_tensor("crowE", [1, 2 * 512], F32))
            w1b = E1(nc.sbuf_tensor("w1b", [1, 128], F32))
            xb2 = E1(nc.sbuf_tensor("xb2", [128, NF * 1024], BF16))
            sqE = E1(nc.sbuf_tensor("sqE", [128, 4 * 512], BF16))

            @bk.gpsimd
            def _(g, sn=snap()):
                waitall(g, sn)
                src = ap["x2_dram"].rearrange("(c p) t -> p c t", p=128)
                i = g.dma_start(out=xb2[:].rearrange("p (c w) -> p c w", c=NF), in_=src)
                sG.inc(i)

            for cg in range(2):
                s0 = 512 * cg

                @bk.tensor
                def _(t, s0=s0, sn=snap()):
                    waitall(t, sn)
                    for fc in range(NF):
                        i = t.matmul(pC[0:1, 0:512], ones_col[:],
                                     xb2[:, fc * 1024 + s0:fc * 1024 + s0 + 512],
                                     start=(fc == 0), stop=(fc == NF - 1))
                    sP.inc(i)

                for blk in range(8):
                    @bk.gpsimd
                    def _(g, s0=s0, blk=blk, sn=snap()):
                        waitall(g, sn)
                        src = (xb2[:].rearrange("p (c t) -> p c t", c=NF)
                               [:, 4 * blk:4 * blk + 4, s0:s0 + 512])
                        dst = sqE[:].rearrange("p (c t) -> p c t", c=4)
                        i = g.tensor_mul(dst, src, src)
                        sO.inc(i)

                    @bk.tensor
                    def _(t, blk=blk, sn=snap()):
                        waitall(t, sn)
                        for f4 in range(4):
                            i = t.matmul(pB[0:1, 0:512], ones_col[:],
                                         sqE[:, f4 * 512:(f4 + 1) * 512],
                                         start=(blk == 0 and f4 == 0),
                                         stop=(blk == 7 and f4 == 3))
                        sP.inc(i)

                sd = stats_math(pC[0:1, 0:512], pB[0:1, 0:512], 512, D)

                @bk.tensor
                def _(t, sd=sd):
                    t.wait_ge(sV.h, sd)
                    i = t.matmul(pB[:, 512:1024], onesf_row[:], arow[:, 0:512],
                                 start=True, stop=True)
                    sP.inc(i)

                @bk.vector
                def _(v, cg=cg, sn=snap()):
                    waitall(v, sn)
                    v.tensor_copy(a_bE[:, 512 * cg:512 * cg + 512], pB[:, 512:1024])
                    i = v.tensor_copy(crowE[:, 512 * cg:512 * cg + 512], crow[:, 0:512])
                    sV.inc(i)

            qkv_pe_mark = {}
            qkv_epi_mark = 0
            for oc in range(48):
                slot = oc % 2

                @bk.gpsimd
                def _(g, oc=oc, slot=slot, m=qkv_pe_mark.get(oc - 2, 0),
                      sn=(snap() if oc == 0 else None)):
                    if sn is not None:
                        waitall(g, sn)
                    if m:
                        g.wait_ge(sP.h, m)
                    src = ap["wb_qkvT"].rearrange("(c p) o -> p c o", p=128)
                    i = g.dma_start(out=wqbB[:, slot * 4096:(slot + 1) * 4096]
                                    .rearrange("p (c o) -> p c o", c=NF),
                                    in_=src[:, :, 128 * oc:128 * (oc + 1)])
                    sG.inc(i)
                ldq = sG.n

                @bk.tensor
                def _(t, oc=oc, slot=slot, ldq=ldq, em=qkv_epi_mark):
                    t.wait_ge(sG.h, ldq)
                    wsl = wqbB[:, slot * 4096:(slot + 1) * 4096]
                    for fc in range(NF):
                        i = t.matmul(pC[0:1, 0:128], ones_col[:],
                                     wsl[fc * 128:(fc + 1) * 128] if False else
                                     wsl[:, fc * 128:(fc + 1) * 128],
                                     start=(fc == 0), stop=(fc == NF - 1))
                    sP.inc(i)
                w1_p = sP.n

                @bk.vector
                def _(v, p=w1_p):
                    v.wait_ge(sP.h, p)
                    i = v.tensor_copy(w1b[:], pC[0:1, 0:128])
                    sV.inc(i)
                w1_v = sV.n

                @bk.tensor
                def _(t, oc=oc, slot=slot, em=qkv_epi_mark, wv=w1_v):
                    if em:
                        t.wait_ge(sV.h, em)
                    wsl = wqbB[:, slot * 4096:(slot + 1) * 4096]
                    for cg in range(2):
                        bank = pA[:, 1024 * cg:1024 * cg + 512]
                        for fc in range(NF):
                            t.matmul(bank, wsl[:, fc * 128:(fc + 1) * 128],
                                     xb2[:, fc * 1024 + 512 * cg:fc * 1024 + 512 * cg + 512],
                                     start=(fc == 0), stop=False)
                    t.wait_ge(sV.h, wv)
                    for cg in range(2):
                        bank = pA[:, 1024 * cg:1024 * cg + 512]
                        i = t.matmul(bank, w1b[:], crowE[:, 512 * cg:512 * cg + 512],
                                     start=False, stop=True)
                    sP.inc(i)
                qkv_pe_mark[oc] = sP.n
                mains_p = sP.n

                @bk.vector
                def _(v, oc=oc, p=mains_p):
                    v.wait_ge(sP.h, p)
                    for cg in range(2):
                        bank = pA[:, 1024 * cg:1024 * cg + 512]
                        i = v.tensor_tensor(
                            out=qb[:, oc * 1024 + 512 * cg:oc * 1024 + 512 * cg + 512],
                            in0=bank, in1=a_bE[:, 512 * cg:512 * cg + 512], op=ALU.mult)
                    sV.inc(i)
                qkv_epi_mark = sV.n

            barrier()
            if "erows" in dbg:
                @bk.sync
                def _(sy):
                    i = sy.dma_start(out=ap["dbg_erows"][0:1, 0:128], in_=w1b[:])
                    sS.inc(i)
                    i = sy.dma_start(out=ap["dbg_erows"][1:2, 0:1024], in_=crowE[:])
                    sS.inc(i)
                barrier()
            if "qb" in dbg:
                @bk.sync
                def _(sy):
                    i = sy.dma_start(out=ap["dbg_qb"], in_=qb[:])
                    sS.inc(i)
                barrier()
            stE1.close()
            AOB = EE(nc.sbuf_tensor("AOB", [128, NF * 1024], BF16))
            stE2 = contextlib.ExitStack()
            E2 = stE2.enter_context
            hsel_s = E2(nc.sbuf_tensor("hsel_s", [128, 1024], BF16))
            seh_s = E2(nc.sbuf_tensor("seh_s", [32, 4096], BF16))
            prod2 = E2(nc.sbuf_tensor("prod2", [128, 2 * 256], BF16))
            esE = E2(nc.sbuf_tensor("esE", [32, 4 * 256], F32))
            mxE = E2(nc.sbuf_tensor("mxE", [32, 256], F32))
            rlE = E2(nc.sbuf_tensor("rlE", [32, 256], F32))
            pbsE = E2(nc.sbuf_tensor("pbsE", [32, 4 * 256], BF16))
            pbb4 = E2(nc.sbuf_tensor("pbb4", [128, 2 * 4 * 256], BF16))
            vtmp = E2(nc.sbuf_tensor("vtmp", [128, 256], F32))
            accE = E2(nc.sbuf_tensor("accE", [128, 256], F32))

            @bk.sync
            def _(sy, sn=snap()):
                waitall(sy, sn)
                sW.inc(sy.dma_start(out=hsel_s[:], in_=ap["hsel"]))
                sW.inc(sy.dma_start(out=seh_s[:], in_=ap["seh"]))

            for i_s in range(4 if doEm else 0):
                for j_s in range(i_s + 1):
                    pm = {}
                    for hc in range(32):
                        slot = hc % 2

                        @bk.vector
                        def _(v, i_s=i_s, j_s=j_s, hc=hc, slot=slot,
                              m=pm.get(hc - 2, 0), sn=(snap() if hc == 0 else None)):
                            if sn is not None:
                                waitall(v, sn)
                            if m:
                                v.wait_ge(sP.h, m)
                            g_ = hc // 4
                            qsl = qb[:, hc * 1024 + i_s * 256:hc * 1024 + i_s * 256 + 256]
                            ksl_ = qb[:, (32 + g_) * 1024 + j_s * 256:
                                       (32 + g_) * 1024 + j_s * 256 + 256]
                            i = v.tensor_tensor(out=prod2[:, slot * 256:slot * 256 + 256],
                                                in0=qsl, in1=ksl_, op=ALU.mult)
                            sV.inc(i)
                        pv = sV.n

                        @bk.tensor
                        def _(t, j_s=j_s, hc=hc, slot=slot, pv=pv):
                            t.wait_ge(sV.h, pv)
                            i = t.matmul(pA[0:32, 512 * j_s:512 * j_s + 256],
                                         hsel_s[:, 32 * hc:32 * hc + 32],
                                         prod2[:, slot * 256:slot * 256 + 256],
                                         start=(hc == 0), stop=(hc == 31))
                            sP.inc(i)
                        pm[hc] = sP.n

                @bk.vector
                def _(v, i_s=i_s, sn=snap()):
                    waitall(v, sn)
                    v.tensor_copy(mxE[:], pA[0:32, 0:256])
                    for j in range(1, i_s + 1):
                        v.tensor_tensor(out=mxE[:], in0=pA[0:32, 512 * j:512 * j + 256],
                                        in1=mxE[:], op=ALU.max)
                    for j in range(i_s + 1):
                        i = v.tensor_tensor(out=esE[:, 256 * j:256 * (j + 1)],
                                            in0=pA[0:32, 512 * j:512 * j + 256],
                                            in1=mxE[:], op=ALU.subtract)
                    sV.inc(i)
                es_v = sV.n

                @bk.scalar
                def _(sc, i_s=i_s, ev=es_v):
                    sc.wait_ge(sV.h, ev)
                    i = sc.activation(out=esE[:, 0:256 * (i_s + 1)],
                                      in_=esE[:, 0:256 * (i_s + 1)], func=AF.Exp,
                                      scale=1.0 / float(np.sqrt(128.0)))
                    sA.inc(i)
                exp_a = sA.n

                @bk.vector
                def _(v, i_s=i_s, ea=exp_a):
                    v.wait_ge(sA.h, ea)
                    v.tensor_copy(rlE[:], esE[:, 0:256])
                    for j in range(1, i_s + 1):
                        v.tensor_tensor(out=rlE[:], in0=esE[:, 256 * j:256 * (j + 1)],
                                        in1=rlE[:], op=ALU.add)
                    v.reciprocal(rlE[:], rlE[:])
                    for j in range(i_s + 1):
                        i = v.tensor_tensor(out=pbsE[:, 256 * j:256 * (j + 1)],
                                            in0=esE[:, 256 * j:256 * (j + 1)],
                                            in1=rlE[:], op=ALU.mult)
                    sV.inc(i)

                pbs_done = sV.n
                av_ev = {}
                av_pool = {}
                for hc in range(32):
                    par = hc % 2
                    bank = (pA if par == 0 else pB)

                    @bk.tensor
                    def _(t, i_s=i_s, hc=hc, bank=bank,
                          evm=av_ev.get(hc - 2, 0), pd=pbs_done):
                        if hc < 2:
                            t.wait_ge(sV.h, pd)
                        if evm:
                            t.wait_ge(sV.h, evm)
                        for j in range(i_s + 1):
                            i = t.matmul(bank[:, 256 * j:256 * j + 256],
                                         seh_s[:, 128 * hc:128 * (hc + 1)],
                                         pbsE[0:32, 256 * j:256 * (j + 1)],
                                         start=True, stop=True)
                        sP.inc(i)
                    bc_p = sP.n

                    @bk.vector
                    def _(v, i_s=i_s, bank=bank, par=par, bp=bc_p,
                          om=av_pool.get(hc - 2, 0)):
                        v.wait_ge(sP.h, bp)
                        if om:
                            v.wait_ge(sO.h, om)
                        for j in range(i_s + 1):
                            i = v.tensor_copy(pbb4[:, par * 1024 + 256 * j:
                                                   par * 1024 + 256 * (j + 1)],
                                              bank[:, 256 * j:256 * j + 256])
                        sV.inc(i)
                    av_ev[hc] = sV.n
                    bb_v = sV.n

                    @bk.gpsimd
                    def _(g, i_s=i_s, hc=hc, par=par, bv=bb_v):
                        g.wait_ge(sV.h, bv)
                        g_ = hc // 4
                        for j in range(i_s + 1):
                            vsl_ = qb[:, (40 + g_) * 1024 + j * 256:
                                      (40 + g_) * 1024 + j * 256 + 256]
                            pb = pbb4[:, par * 1024 + 256 * j:par * 1024 + 256 * (j + 1)]
                            g.tensor_mul(vtmp[:], vsl_, pb)
                            if j == 0:
                                i = g.tensor_copy(accE[:], vtmp[:])
                            else:
                                i = g.tensor_add(accE[:], accE[:], vtmp[:])
                        i = g.tensor_copy(
                            AOB[:, hc * 1024 + i_s * 256:hc * 1024 + i_s * 256 + 256],
                            accE[:])
                        sO.inc(i)
                    av_pool[hc] = sO.n

            barrier()
            if "AOB" in dbg:
                @bk.sync
                def _(sy):
                    i = sy.dma_start(out=ap["dbg_AOB"], in_=AOB[:])
                    sS.inc(i)
                barrier()
            stE2.close()
            xrE = EE(nc.sbuf_tensor("xrE", [128, 2 * 512], F32))
            wqbP = EE(nc.sbuf_tensor("wqbP", [128, NF * 128], BF16))

            for oc in range(NF if doEp else 0):
                @bk.gpsimd
                def _(g, oc=oc, sn=snap()):
                    waitall(g, sn)
                    src = ap["wb_projT"].rearrange("(c p) o -> p c o", p=128)
                    i = g.dma_start(out=wqbP[:].rearrange("p (c o) -> p c o", c=NF),
                                    in_=src[:, :, 128 * oc:128 * (oc + 1)])
                    sG.inc(i)

                @bk.sync
                def _(sy, oc=oc, sn=snap()):
                    waitall(sy, sn)
                    sW.inc(sy.dma_start(out=xrE[:, 0:512],
                                        in_=ap["x2_dram"][128 * oc:128 * (oc + 1), 0:512]))
                    sW.inc(sy.dma_start(out=xrE[:, 512:1024],
                                        in_=ap["x2_dram"][128 * oc:128 * (oc + 1), 512:1024]))

                for cg in range(2):
                    @bk.tensor
                    def _(t, oc=oc, cg=cg, sn=snap()):
                        waitall(t, sn)
                        bank = pA[:, 1024 * cg:1024 * cg + 512]
                        for fc in range(NF):
                            i = t.matmul(bank, wqbP[:, fc * 128:(fc + 1) * 128],
                                         AOB[:, fc * 1024 + 512 * cg:fc * 1024 + 512 * cg + 512],
                                         start=(fc == 0), stop=(fc == NF - 1))
                        sP.inc(i)

                    @bk.vector
                    def _(v, oc=oc, cg=cg, sn=snap()):
                        waitall(v, sn)
                        bank = pA[:, 1024 * cg:1024 * cg + 512]
                        i = v.tensor_tensor(out=xrE[:, 512 * cg:512 * cg + 512],
                                            in0=bank, in1=xrE[:, 512 * cg:512 * cg + 512],
                                            op=ALU.add)
                        sV.inc(i)

                    @bk.sync
                    def _(sy, oc=oc, cg=cg, sn=snap()):
                        waitall(sy, sn)
                        i = sy.dma_start(out=ap["x3_dram"][128 * oc:128 * (oc + 1),
                                                           512 * cg:512 * cg + 512],
                                         in_=xrE[:, 512 * cg:512 * cg + 512])
                        sS.inc(i)

            barrier()
            stE.close()

        if "D2" in phases:
            ffn_phase("x3_dram", "y")

        barrier()
        dram_dbg = [n for n in dbg if n in ("K_dram", "V_dram", "x1_dram",
                                            "x2_dram", "x3_dram")]
        if dram_dbg:
            @bk.sync
            def _(sy):
                for name in dram_dbg:
                    i = sy.dma_start(out=ap["dbg_" + name], in_=ap[name])
                    sS.inc(i)

        @bk.sync
        def _(sy):
            sy.wait_ge(sS.h, sS.n)
    return nc, ap


def kernel(**inputs):
    shared, percore = host_prep(inputs)
    nc, _ = build_nc(dbg=tuple(_DBG))
    in_maps = []
    for j in range(NCORE):
        m = dict(shared)
        m.update(percore[j])
        in_maps.append(m)
    res = run_bass_kernel_spmd(nc, in_maps, core_ids=list(range(NCORE)))
    if _DBG:
        kernel.dbg_out = [{k: v for k, v in r.items() if k.startswith("dbg_")}
                          for r in res.results]
    parts = np.stack([res.results[j]["y"] for j in range(NCORE)])
    y = parts.reshape(NCORE, D, B, OWNP).transpose(2, 0, 3, 1).reshape(B, S, D)
    return np.ascontiguousarray(y)
